# version 02
# baseline (speedup 1.0000x reference)
import numpy as np
import concourse.bass as bass
import concourse.mybir as mybir
from concourse.bass_utils import run_bass_kernel_spmd
from contextlib import ExitStack

F32 = mybir.dt.float32
BF16 = mybir.dt.bfloat16
AF = mybir.ActivationFunctionType
OP = mybir.AluOpType

C = 2048
T = 8192
NCORES = 8
TPC = T // NCORES
KC = 16
TA = 256
NSA = TPC // TA
WARM = 128
TF = 512
NSF = TPC // TF
H = 8192
NH = H // 128
EPS = 1e-5
ISQ = float(1.0 / np.sqrt(8192.0))
XCOLS = 1 + WARM + TPC


class Sem:
    def __init__(self, h, name, step=1):
        self.h, self.name, self.n, self.step = h, name, 0, step

    def tick(self):
        self.n += self.step
        return self.n


class Em:
    ENGS = ("sync", "gpsimd", "tensor", "vector", "scalar")

    def __init__(self):
        self.streams = {k: [] for k in self.ENGS}
        self.wm = {k: {} for k in self.ENGS}

    def op(self, eng, fn, waits=(), inc=None):
        wl = []
        wm = self.wm[eng]
        for sw in waits:
            if sw is None:
                continue
            sem, val = sw
            if val is None or val <= 0:
                continue
            if val > wm.get(sem.name, 0):
                wm[sem.name] = val
                wl.append((sem, val))

        def run(e, fn=fn, wl=wl, inc=inc):
            for sem, val in wl:
                e.wait_ge(sem.h, val)
            ins = fn(e)
            if inc is not None:
                sem, k = inc
                ins.then_inc(sem.h, k)
        self.streams[eng].append(run)

    def wait(self, eng, waits):
        wl = []
        wm = self.wm[eng]
        for sem, val in waits:
            if val and val > wm.get(sem.name, 0):
                wm[sem.name] = val
                wl.append((sem, val))
        if wl:
            def run(e, wl=wl):
                for sem, val in wl:
                    e.wait_ge(sem.h, val)
            self.streams[eng].append(run)


import os
KDBG = os.environ.get("KDBG", "")


def build_nc():
    nc = bass.Bass("TRN2", target_bir_lowering=False)
    xT = nc.dram_tensor("xT", [C, XCOLS], F32, kind="ExternalInput").ap()
    WrT = nc.dram_tensor("WrT", [C, C], F32, kind="ExternalInput").ap()
    WkT = nc.dram_tensor("WkT", [C, C], F32, kind="ExternalInput").ap()
    WvT = nc.dram_tensor("WvT", [C, C], F32, kind="ExternalInput").ap()
    WgT = nc.dram_tensor("WgT", [C, C], F32, kind="ExternalInput").ap()
    WoT = nc.dram_tensor("WoT", [C, C], F32, kind="ExternalInput").ap()
    W1T = nc.dram_tensor("W1T", [C, H], F32, kind="ExternalInput").ap()
    W3T = nc.dram_tensor("W3T", [C, H], F32, kind="ExternalInput").ap()
    W2T = nc.dram_tensor("W2T", [H, C], F32, kind="ExternalInput").ap()
    PAR = nc.dram_tensor("PAR", [128, 129], F32, kind="ExternalInput").ap()
    HAD = nc.dram_tensor("HAD", [128, 128], F32, kind="ExternalInput").ap()
    OUT = nc.dram_tensor("OUT", [C, TPC], F32, kind="ExternalOutput").ap()
    X1SP = nc.dram_tensor("X1SP", [C, TPC], BF16, kind="Internal").ap()

    xTr = xT.rearrange("(c p) t -> p c t", p=128)
    outR = OUT.rearrange("(c p) t -> p c t", p=128)
    x1R = X1SP.rearrange("(c p) t -> p c t", p=128)
    W1R = W1T.rearrange("(c p) h -> p c h", p=128)
    W3R = W3T.rearrange("(c p) h -> p c h", p=128)
    W2R = W2T.rearrange("(c p) n -> p c n", p=128)

    em = Em()
    dbg_state = {"ap": None, "gate": None, "used": False}
    if KDBG:
        dbg_state["ap"] = nc.dram_tensor("DBG", [128, 16, 512], F32, kind="ExternalOutput").ap()
    with ExitStack() as stack:
        ent = stack.enter_context
        def dsem(name):
            return Sem(ent(nc.semaphore(name)), name, 16)
        wringS = [dsem("A%d" % i) for i in range(4)]
        w1S = [dsem("B%d" % i) for i in range(2)]
        w3S = [dsem("B3%d" % i) for i in range(2)]
        w2S = [dsem("C%d" % i) for i in range(5)]
        xtS = dsem("XT")
        xfS = dsem("XF")
        x1S = dsem("X1")
        spS = dsem("SP")
        outS = dsem("OU")
        parS = dsem("PA")
        hadS = dsem("HA")
        dbgS = dsem("DB")
        ST = Sem(ent(nc.semaphore("ST")), "ST")
        SV = Sem(ent(nc.semaphore("SV")), "SV")
        SS = Sem(ent(nc.semaphore("SS")), "SS")

        def tick(sem):
            v = sem.tick()
            return (sem, sem.step), v

        def dbg_dump(tag, src_fn, waits, cast=False, n=256):
            if KDBG != tag or dbg_state["used"]:
                return
            dbg_state["used"] = True
            _, v = tick(dbgS)
            em.op("gpsimd" if cast else "sync",
                  lambda e: e.dma_start(dbg_state["ap"][:, :, 0:n], src_fn()),
                  waits=waits, inc=(dbgS, 16))
            dbg_state["gate"] = (dbgS, v)

        def dbg_gate():
            g = dbg_state["gate"]
            return [g] if g is not None else []

        ps = ent(nc.psum_tensor("ps", [128, 4096], F32))
        h128b = ent(nc.sbuf_tensor("h128b", [128, 128], BF16))
        par = ent(nc.sbuf_tensor("par", [128, 129], F32))
        cns = ent(nc.sbuf_tensor("cns", [128, 48], F32))
        zeros = ent(nc.sbuf_tensor("zeros", [128, 256], F32))
        w_rep = ent(nc.sbuf_tensor("w_rep", [128, 256], F32))
        carA = ent(nc.sbuf_tensor("carA", [128, 16], F32))
        carB = ent(nc.sbuf_tensor("carB", [128, 16], F32))
        statf = ent(nc.sbuf_tensor("statf", [1, 4, 512], F32))
        statb = ent(nc.sbuf_tensor("statb", [1, 2, 512], BF16))
        lnrA = ent(nc.sbuf_tensor("lnrA", [128, 2, 512], BF16))
        lnrB = ent(nc.sbuf_tensor("lnrB", [128, 2, 512], BF16))

        tm = {k: par[:, 16 * i:16 * (i + 1)] for i, k in
              enumerate(["k", "v", "r", "g", "td", "tf", "lng", "lnb"])}
        w_c = cns[:, 0:16]
        eu_c = cns[:, 16:32]
        c2_c = cns[:, 32:48]

        wmsk = par[:, 128:129]
        _, vPAR = tick(parS)
        em.op("sync", lambda e: e.dma_start(par[:, :], PAR), inc=(parS, 16))
        vWM = vPAR
        _, vHAD = tick(hadS)
        em.op("gpsimd", lambda e: e.dma_start(h128b[:, :], HAD), inc=(hadS, 16))
        ones_col = h128b[:, 0:1]
        ones_row = h128b[0:1, :]

        em.op("scalar", lambda e: e.activation(cns[:, 0:16], tm["td"], AF.Exp),
              waits=[(parS, vPAR)])
        em.op("scalar", lambda e: e.activation(cns[:, 0:16], cns[:, 0:16], AF.Exp, scale=-1.0))
        inc, vEU = tick(SS)
        em.op("scalar", lambda e: e.activation(cns[:, 16:32], tm["tf"], AF.Exp), inc=inc)
        em.op("vector", lambda e: e.tensor_mul(cns[:, 32:48], cns[:, 0:16], cns[:, 16:32]),
              waits=[(SS, vEU)])
        inc, vZ = tick(SV)
        em.op("vector", lambda e: e.memset(zeros[:, :], 0.0), inc=inc)
        em.op("vector", lambda e: e.memset(w_rep[:, :], 0.0))
        inc, vC2 = tick(SV)
        em.op("vector", lambda e: e.tensor_scalar(cns[:, 32:48], cns[:, 32:48], -1.0, 1.0,
                                                  OP.mult, OP.add), inc=inc)

        bank_gate = [[] for _ in range(8)]
        last_ln_roll = [0]

        def ln_phase(src_fn, N, st_cols, bc_cols, out_writer, src_ready):
            cp_vals = [0] * 16
            rfv = [0] * 16
            for mo in range(16):
                w = [src_ready[mo]] if src_ready else []
                if mo >= 2:
                    w.append((ST, rfv[mo - 2]))
                else:
                    w.append((ST, last_ln_roll[0]))
                em.op("scalar",
                      lambda e, mo=mo: e.activation(lnrA[:, mo % 2, 0:N], src_fn(mo), AF.Copy),
                      waits=w)
                inc, cp_vals[mo] = tick(SS)
                em.op("scalar",
                      lambda e, mo=mo: e.activation(lnrB[:, mo % 2, 0:N], src_fn(mo), AF.Square),
                      inc=inc)
            for mo in range(16):
                waits = [(SS, cp_vals[mo])]
                if mo == 0:
                    waits += bank_gate[st_cols[0] // 512] + bank_gate[st_cols[1] // 512]
                sb_same = (st_cols[0] // 512) == (st_cols[1] // 512)
                em.op("tensor",
                      lambda e, mo=mo: e.matmul(ps[0:1, st_cols[0]:st_cols[0] + N], ones_col,
                                                lnrA[:, mo % 2, 0:N],
                                                start=(mo == 0), stop=(mo == 15)),
                      waits=waits)
                inc, rfv[mo] = tick(ST)
                em.op("tensor",
                      lambda e, mo=mo, sb_same=sb_same: e.matmul(
                          ps[0:1, st_cols[1]:st_cols[1] + N], ones_col,
                          lnrB[:, mo % 2, 0:N],
                          start=(mo == 0 and not sb_same), stop=(mo == 15)),
                      inc=inc)
            last_ln_roll[0] = rfv[15]
            st_stop = rfv[15]
            em.op("vector",
                  lambda e: e.tensor_scalar(statf[0:1, 0, 0:N], ps[0:1, st_cols[0]:st_cols[0] + N],
                                            1.0 / 2048.0, None, OP.mult),
                  waits=[(ST, st_stop)])
            em.op("vector",
                  lambda e: e.tensor_scalar(statf[0:1, 1, 0:N], ps[0:1, st_cols[1]:st_cols[1] + N],
                                            1.0 / 2048.0, EPS, OP.mult, OP.add))
            em.op("vector", lambda e: e.tensor_mul(statf[0:1, 2, 0:N], statf[0:1, 0, 0:N],
                                                   statf[0:1, 0, 0:N]))
            inc, vmunge = tick(SV)
            em.op("vector", lambda e: e.tensor_sub(statf[0:1, 1, 0:N], statf[0:1, 1, 0:N],
                                                   statf[0:1, 2, 0:N]), inc=inc)
            bank_gate[st_cols[0] // 512] = [(SV, vmunge)]
            bank_gate[st_cols[1] // 512] = [(SV, vmunge)]
            em.op("scalar",
                  lambda e: e.activation(statf[0:1, 3, 0:N], statf[0:1, 1, 0:N], AF.Ln),
                  waits=[(SV, vmunge)])
            em.op("scalar", lambda e: e.activation(statf[0:1, 3, 0:N], statf[0:1, 3, 0:N],
                                                   AF.Exp, scale=-0.5))
            em.op("scalar", lambda e: e.activation(statb[0:1, 0, 0:N], statf[0:1, 0, 0:N], AF.Copy))
            inc, vcast = tick(SS)
            em.op("scalar", lambda e: e.activation(statb[0:1, 1, 0:N], statf[0:1, 3, 0:N], AF.Copy),
                  inc=inc)
            em.op("tensor",
                  lambda e: e.matmul(ps[:, bc_cols[0]:bc_cols[0] + N], ones_row, statb[0:1, 0, 0:N],
                                     start=True, stop=True),
                  waits=[(SS, vcast), (hadS, vHAD)] + bank_gate[bc_cols[0] // 512])
            inc, vbc = tick(ST)
            bc_same = (bc_cols[0] // 512) == (bc_cols[1] // 512)
            em.op("tensor",
                  lambda e, bc_same=bc_same: e.matmul(
                      ps[:, bc_cols[1]:bc_cols[1] + N], ones_row, statb[0:1, 1, 0:N],
                      start=(not bc_same), stop=True),
                  waits=bank_gate[bc_cols[1] // 512], inc=inc)
            mu_b = ps[:, bc_cols[0]:bc_cols[0] + N]
            rs_b = ps[:, bc_cols[1]:bc_cols[1] + N]
            last = out_writer(mu_b, rs_b, (ST, vbc))
            bank_gate[bc_cols[0] // 512] = [(SV, last)]
            bank_gate[bc_cols[1] // 512] = [(SV, last)]

        astack = ExitStack()
        aent = astack.enter_context
        xtile = aent(nc.sbuf_tensor("xtile", [128, 16, 385], BF16))
        xf32 = aent(nc.sbuf_tensor("xf32", [128, 16, 256], F32))
        xx_r = aent(nc.sbuf_tensor("xx_r", [128, 256], F32))
        stg_k = aent(nc.sbuf_tensor("stg_k", [128, 16, 256], BF16))
        stg_v = aent(nc.sbuf_tensor("stg_v", [128, 16, 256], BF16))
        stg_r = aent(nc.sbuf_tensor("stg_r", [128, 16, 256], BF16))
        stg_g = aent(nc.sbuf_tensor("stg_g", [128, 16, 256], BF16))
        ekT = aent(nc.sbuf_tensor("ekT", [128, 16, 256], F32))
        uT = aent(nc.sbuf_tensor("uT", [128, 16, 256], F32))
        apg = aent(nc.sbuf_tensor("apg", [128, 2, 257], F32))
        bpg = aent(nc.sbuf_tensor("bpg", [128, 2, 257], F32))
        numb = aent(nc.sbuf_tensor("numb", [128, 2, 256], F32))
        denb = aent(nc.sbuf_tensor("denb", [128, 2, 256], F32))
        invb = aent(nc.sbuf_tensor("invb", [128, 2, 256], F32))
        rsig = aent(nc.sbuf_tensor("rsig", [128, 16, 256], BF16))
        gsr = aent(nc.sbuf_tensor("gsr", [128, 2, 256], BF16))
        wring = aent(nc.sbuf_tensor("wring", [128, 4, 2, 2048], BF16))
        attn_b16 = stg_r
        x1sub = stg_g

        wring_gate = [[] for _ in range(4)]
        spill_vals = []

        def proj(wdram, rhs_fn, N, rhs_ready):
            wR = wdram.rearrange("(c p) n -> p c n", p=128)
            dv = [0] * 8

            def emit_dma(p, waits):
                slot = p % 4
                _, dv[p] = tick(wringS[slot])
                em.op("gpsimd",
                      lambda e, p=p, slot=slot, wR=wR: e.dma_start(
                          wring[:, slot, :, :], wR[:, 2 * p:2 * p + 2, :]),
                      waits=waits, inc=(wringS[slot], 16))

            for p in range(4):
                emit_dma(p, list(wring_gate[p]))
                wring_gate[p] = []
            stop_vals = [0] * 16
            kc_tick = [0] * 16
            for kc in range(16):
                p, i = kc // 2, kc % 2
                slot = p % 4
                for mo in range(16):
                    waits = []
                    if mo == 0:
                        waits.append((wringS[slot], dv[p]))
                        if kc == 0:
                            waits += list(rhs_ready)
                    if kc == 0:
                        waits += bank_gate[mo // 2]
                    inc = None
                    if kc == 15:
                        inc, stop_vals[mo] = tick(ST)
                    elif kc % 2 == 1 and mo == 15:
                        inc, kc_tick[kc] = tick(ST)
                    em.op("tensor",
                          lambda e, kc=kc, i=i, slot=slot, mo=mo, N=N, rhs_fn=rhs_fn: e.matmul(
                              ps[:, 256 * mo:256 * mo + N],
                              wring[:, slot, i, 128 * mo:128 * mo + 128],
                              rhs_fn(kc),
                              start=(kc == 0 and mo % 2 == 0), stop=(kc == 15)),
                          waits=waits, inc=inc)
                if kc in (1, 3, 5, 7):
                    emit_dma(4 + (kc - 1) // 2, [(ST, kc_tick[kc])])
            for b in range(8):
                bank_gate[b] = []
            wring_gate[0] = [(ST, kc_tick[9])]
            wring_gate[1] = [(ST, kc_tick[11])]
            wring_gate[2] = [(ST, kc_tick[13])]
            wring_gate[3] = [(ST, stop_vals[15])]
            return stop_vals

        def lerp_phase(base, n, sigs, xt_ready, extra=()):
            fin = 0
            for kc in range(16):
                w = list(extra) + [xt_ready] if kc == 0 else []
                em.op("vector",
                      lambda e, kc=kc, base=base, n=n: e.tensor_sub(
                          xx_r[:, 0:n], xtile[:, kc, base:base + n],
                          xtile[:, kc, base + 1:base + 1 + n]),
                      waits=w)
                for name, dst in sigs:
                    inc, fin = tick(SV)
                    em.op("vector",
                          lambda e, kc=kc, base=base, n=n, name=name, dst=dst:
                          e.scalar_tensor_tensor(
                              dst[:, kc, 0:n], xx_r[:, 0:n], tm[name][:, kc:kc + 1],
                              xtile[:, kc, base + 1:base + 1 + n], OP.mult, OP.add),
                          inc=inc)
            return fin

        _, vXT = tick(xtS)
        em.op("gpsimd", lambda e: e.dma_start(xtile[:, :, 0:385], xTr[:, :, 0:385]),
              inc=(xtS, 16))
        xt_ready = (xtS, vXT)

        lerp_fin = lerp_phase(0, 128, [("k", stg_k), ("v", stg_v)], xt_ready)
        kstops = proj(WkT, lambda kc: stg_k[:, kc, 0:128], 128, [(SV, lerp_fin)])
        ek_vals = [0] * 16
        for mo in range(16):
            inc, ek_vals[mo] = tick(SS)
            em.op("scalar",
                  lambda e, mo=mo: e.activation(ekT[:, mo, 0:128],
                                                ps[:, 256 * mo:256 * mo + 128], AF.Exp),
                  waits=[(ST, kstops[mo | 1])], inc=inc)
            bank_gate[mo // 2] = [(SS, ek_vals[mo])]
        vstops = proj(WvT, lambda kc: stg_v[:, kc, 0:128], 128, [(SV, lerp_fin)])
        u_vals = [0] * 16
        for mo in range(16):
            inc, u_vals[mo] = tick(SV)
            em.op("vector",
                  lambda e, mo=mo: e.tensor_mul(uT[:, mo, 0:128],
                                                ps[:, 256 * mo:256 * mo + 128],
                                                ekT[:, mo, 0:128]),
                  waits=[(ST, vstops[mo | 1]), (SS, ek_vals[mo])], inc=inc)
            bank_gate[mo // 2] = [(SV, u_vals[mo])]
        def pro_stage(mo):
            g = mo % 2
            em.op("vector",
                  lambda e, mo=mo: e.scalar_tensor_tensor(
                      w_rep[:, 0:128], zeros[:, 0:128], w_c[:, mo:mo + 1],
                      zeros[:, 0:128], OP.add, OP.add),
                  waits=[(SV, u_vals[mo]), (SV, vZ), (SV, vC2)])
            em.op("vector",
                  lambda e, mo=mo, g=g: e.tensor_tensor_scan(
                      apg[:, g, 0:128], w_rep[:, 0:128], uT[:, mo, 0:128], 0.0,
                      OP.mult, OP.add))
            em.op("vector",
                  lambda e, mo=mo, g=g: e.tensor_tensor_scan(
                      bpg[:, g, 0:128], w_rep[:, 0:128], ekT[:, mo, 0:128], 0.0,
                      OP.mult, OP.add))

        def pro_consume(mo):
            g = mo % 2
            em.op("vector", lambda e, mo=mo, g=g: e.tensor_copy(carA[:, mo:mo + 1],
                                                                apg[:, g, 127:128]))
            em.op("vector", lambda e, mo=mo, g=g: e.tensor_copy(carB[:, mo:mo + 1],
                                                                bpg[:, g, 127:128]))

        pro_stage(0)
        for mo in range(1, 16):
            pro_stage(mo)
            pro_consume(mo - 1)
        pro_consume(15)
        em.op("vector", lambda e: e.tensor_scalar(carA[:, :], carA[:, :], wmsk[:, 0:1],
                                                  None, OP.mult),
              waits=[(parS, vWM)])
        inc, vCAR = tick(SV)
        em.op("vector", lambda e: e.tensor_scalar(carB[:, :], carB[:, :], wmsk[:, 0:1],
                                                  None, OP.mult), inc=inc)

        prev_z_fin = 0
        prev_attnmul_fin = vCAR
        prev_uT_guard = vCAR
        prev_lerp_fin = lerp_fin
        pro_stop = vstops[15]
        for s in range(NSA):
            base = 128 if s == 0 else 0
            if s > 0:
                _, vXT = tick(xtS)
                em.op("gpsimd",
                      lambda e, s=s: e.dma_start(xtile[:, :, 0:257],
                                                 xTr[:, :, 128 + 256 * s:385 + 256 * s]),
                      waits=[(SV, prev_lerp_fin)], inc=(xtS, 16))
                xt_ready = (xtS, vXT)
            _, vXF = tick(xfS)
            em.op("sync",
                  lambda e, s=s: e.dma_start(xf32[:, :, :],
                                             xTr[:, :, 129 + 256 * s:385 + 256 * s]),
                  waits=[(SV, prev_z_fin)], inc=(xfS, 16))
            extra = [(ST, pro_stop)] if s == 0 else [(spS, spill_vals[-1])]
            lerp_fin = lerp_phase(base, 256,
                                  [("k", stg_k), ("v", stg_v), ("r", stg_r), ("g", stg_g)],
                                  xt_ready, extra=extra)
            prev_lerp_fin = lerp_fin
            kstops = proj(WkT, lambda kc: stg_k[:, kc, 0:256], 256, [(SV, lerp_fin)])
            ek_vals = [0] * 16
            for mo in range(16):
                inc, ek_vals[mo] = tick(SS)
                em.op("scalar",
                      lambda e, mo=mo: e.activation(ekT[:, mo, 0:256],
                                                    ps[:, 256 * mo:256 * mo + 256], AF.Exp),
                      waits=[(ST, kstops[mo | 1]), (SV, prev_attnmul_fin)], inc=inc)
                bank_gate[mo // 2] = [(SS, ek_vals[mo])]
            vstops = proj(WvT, lambda kc: stg_v[:, kc, 0:256], 256, [(SV, lerp_fin)])
            u_vals = [0] * 16
            for mo in range(16):
                inc, u_vals[mo] = tick(SV)
                em.op("vector",
                      lambda e, mo=mo: e.tensor_mul(uT[:, mo, 0:256],
                                                    ps[:, 256 * mo:256 * mo + 256],
                                                    ekT[:, mo, 0:256]),
                      waits=[(ST, vstops[mo | 1]), (SS, ek_vals[mo]), (SV, prev_uT_guard)],
                      inc=inc)
                bank_gate[mo // 2] = [(SV, u_vals[mo])]
            dbg_dump("stgk", lambda: stg_k[:, :, 0:256].rearrange("p c t -> p c t"),
                     [(SV, lerp_fin)], cast=True)
            dbg_dump("ek", lambda: ekT[:, :, :], [(SS, ek_vals[15])])
            dbg_dump("u", lambda: uT[:, :, :], [(SV, u_vals[15])])
            wkv_vals = [0] * 16
            vinv_hist = [0] * 16

            def sc_stage(mo, s=s):
                g = mo % 2
                em.op("vector",
                      lambda e, mo=mo: e.scalar_tensor_tensor(
                          w_rep[:, 0:256], zeros[:, 0:256], w_c[:, mo:mo + 1],
                          zeros[:, 0:256], OP.add, OP.add),
                      waits=[(SV, u_vals[mo])] + ([(SV, vCAR)] if (s == 0 and mo == 0) else []))
                em.op("vector", lambda e, mo=mo, g=g: e.tensor_copy(apg[:, g, 0:1],
                                                                    carA[:, mo:mo + 1]))
                em.op("vector",
                      lambda e, mo=mo, g=g: e.tensor_tensor_scan(
                          apg[:, g, 1:257], w_rep[:, 0:256], uT[:, mo, 0:256],
                          carA[:, mo:mo + 1], OP.mult, OP.add))
                em.op("vector", lambda e, mo=mo, g=g: e.tensor_copy(bpg[:, g, 0:1],
                                                                    carB[:, mo:mo + 1]))
                em.op("vector",
                      lambda e, mo=mo, g=g: e.tensor_tensor_scan(
                          bpg[:, g, 1:257], w_rep[:, 0:256], ekT[:, mo, 0:256],
                          carB[:, mo:mo + 1], OP.mult, OP.add))

            def sc_consume(mo):
                g = mo % 2
                em.op("vector", lambda e, mo=mo, g=g: e.tensor_copy(carA[:, mo:mo + 1],
                                                                    apg[:, g, 256:257]))
                em.op("vector", lambda e, mo=mo, g=g: e.tensor_copy(carB[:, mo:mo + 1],
                                                                    bpg[:, g, 256:257]))
                em.op("vector",
                      lambda e, mo=mo, g=g: e.tensor_scalar(
                          numb[:, g, :], apg[:, g, 1:257], eu_c[:, mo:mo + 1], None, OP.mult))
                em.op("vector",
                      lambda e, mo=mo, g=g: e.scalar_tensor_tensor(
                          numb[:, g, :], apg[:, g, 0:256], c2_c[:, mo:mo + 1],
                          numb[:, g, :], OP.mult, OP.add))
                em.op("vector",
                      lambda e, mo=mo, g=g: e.tensor_scalar(
                          denb[:, g, :], bpg[:, g, 1:257], eu_c[:, mo:mo + 1], None, OP.mult),
                      waits=([(SS, vinv_hist[mo - 2])] if mo >= 2 else []))
                inc, vden = tick(SV)
                em.op("vector",
                      lambda e, mo=mo, g=g: e.scalar_tensor_tensor(
                          denb[:, g, :], bpg[:, g, 0:256], c2_c[:, mo:mo + 1],
                          denb[:, g, :], OP.mult, OP.add), inc=inc)
                em.op("scalar",
                      lambda e, mo=mo, g=g: e.activation(invb[:, g, :], denb[:, g, :], AF.Ln),
                      waits=[(SV, vden)])
                inc, vinv = tick(SS)
                vinv_hist[mo] = vinv
                em.op("scalar",
                      lambda e, mo=mo, g=g: e.activation(invb[:, g, :], invb[:, g, :],
                                                         AF.Exp, scale=-1.0), inc=inc)
                inc, wkv_vals[mo] = tick(SV)
                em.op("vector",
                      lambda e, mo=mo, g=g: e.tensor_mul(ekT[:, mo, 0:256], numb[:, g, :],
                                                         invb[:, g, :]),
                      waits=[(SS, vinv)] + (dbg_gate() if mo == 0 else []), inc=inc)

            sc_stage(0)
            for mo in range(1, 16):
                sc_stage(mo)
                sc_consume(mo - 1)
            sc_consume(15)
            dbg_dump("wkv", lambda: ekT[:, :, :], [(SV, wkv_vals[15])])
            rstops = proj(WrT, lambda kc: stg_r[:, kc, 0:256], 256, [(SV, lerp_fin)])
            rs_vals = [0] * 16
            for mo in range(16):
                inc, rs_vals[mo] = tick(SS)
                em.op("scalar",
                      lambda e, mo=mo: e.activation(rsig[:, mo, 0:256],
                                                    ps[:, 256 * mo:256 * mo + 256], AF.Sigmoid),
                      waits=[(ST, rstops[mo | 1])], inc=inc)
                bank_gate[mo // 2] = [(SS, rs_vals[mo])]
            am_vals = [0] * 16
            for mo in range(16):
                inc, am_vals[mo] = tick(SV)
                em.op("vector",
                      lambda e, mo=mo: e.tensor_mul(ekT[:, mo, 0:256], ekT[:, mo, 0:256],
                                                    rsig[:, mo, 0:256]),
                      waits=[(SS, rs_vals[mo]), (SV, wkv_vals[mo])] +
                            (dbg_gate() if mo == 0 else []), inc=inc)
            gstops = proj(WgT, lambda kc: stg_g[:, kc, 0:256], 256, [(SV, lerp_fin)])
            ab_vals = [0] * 16
            for mo in range(16):
                inc, vgs = tick(SS)
                em.op("scalar",
                      lambda e, mo=mo: e.activation(gsr[:, mo % 2, :],
                                                    ps[:, 256 * mo:256 * mo + 256], AF.Sigmoid),
                      waits=[(ST, gstops[mo | 1])] +
                            ([(SV, ab_vals[mo - 2])] if mo >= 2 else []), inc=inc)
                bank_gate[mo // 2] = [(SS, vgs)]
                inc, ab_vals[mo] = tick(SV)
                em.op("vector",
                      lambda e, mo=mo: e.tensor_mul(attn_b16[:, mo, 0:256], ekT[:, mo, 0:256],
                                                    gsr[:, mo % 2, :]),
                      waits=[(SS, vgs), (SV, am_vals[mo])], inc=inc)
            prev_attnmul_fin = ab_vals[15]
            dbg_dump("attnin", lambda: attn_b16[:, :, 0:256], [(SV, ab_vals[15])], cast=True)
            ostops = proj(WoT, lambda kc: attn_b16[:, kc, 0:256], 256, [(SV, ab_vals[15])])
            o_vals = [0] * 16
            for mo in range(16):
                inc, o_vals[mo] = tick(SS)
                em.op("scalar",
                      lambda e, mo=mo: e.activation(uT[:, mo, 0:256],
                                                    ps[:, 256 * mo:256 * mo + 256], AF.Copy),
                      waits=[(ST, ostops[mo | 1])], inc=inc)
                bank_gate[mo // 2] = [(SS, o_vals[mo])]
            dbg_dump("o", lambda: uT[:, :, :], [(SS, o_vals[15])])
            z_vals = [0] * 16

            def attn_writer(mu_b, rs_b, bc_ready, vXF=vXF, z_vals=z_vals):
                last = 0
                for mo in range(16):
                    em.op("vector",
                          lambda e, mo=mo, mu_b=mu_b: e.tensor_sub(
                              numb[:, mo % 2, :], uT[:, mo, 0:256], mu_b),
                          waits=[bc_ready] if mo == 0 else [])
                    em.op("vector",
                          lambda e, mo=mo, rs_b=rs_b: e.tensor_mul(
                              numb[:, mo % 2, :], numb[:, mo % 2, :], rs_b))
                    em.op("vector",
                          lambda e, mo=mo: e.tensor_scalar(
                              numb[:, mo % 2, :], numb[:, mo % 2, :],
                              tm["lng"][:, mo:mo + 1], tm["lnb"][:, mo:mo + 1],
                              OP.mult, OP.add))
                    inc, z_vals[mo] = tick(SV)
                    em.op("vector",
                          lambda e, mo=mo: e.tensor_add(uT[:, mo, 0:256], numb[:, mo % 2, :],
                                                        xf32[:, mo, :]),
                          waits=[(xfS, vXF)] + (dbg_gate() if mo == 0 else []), inc=inc)
                    last = z_vals[mo]
                return last

            ln_phase(lambda mo: uT[:, mo, 0:256], 256, (0, 256), (512, 768),
                     attn_writer, [(SS, o_vals[mo]) for mo in range(16)])
            prev_z_fin = z_vals[15]
            x1_vals = [0] * 16

            def ln1_writer(mu_b, rs_b, bc_ready, x1_vals=x1_vals, gstop=gstops[15]):
                last = 0
                for mo in range(16):
                    em.op("vector",
                          lambda e, mo=mo, mu_b=mu_b: e.tensor_sub(
                              numb[:, mo % 2, :], uT[:, mo, 0:256], mu_b),
                          waits=[bc_ready] if mo == 0 else [])
                    inc, x1_vals[mo] = tick(SV)
                    em.op("vector",
                          lambda e, mo=mo, rs_b=rs_b: e.tensor_mul(
                              x1sub[:, mo, 0:256], numb[:, mo % 2, :], rs_b),
                          waits=[(ST, gstop)], inc=inc)
                    last = x1_vals[mo]
                return last

            dbg_dump("z", lambda: uT[:, :, :], [(SV, z_vals[15])])
            ln_phase(lambda mo: uT[:, mo, 0:256], 256, (1024, 1280), (1536, 1792),
                     ln1_writer, [(SV, z_vals[mo]) for mo in range(16)])
            prev_uT_guard = x1_vals[15]
            dbg_dump("x1", lambda: x1sub[:, :, 0:256], [(SV, x1_vals[15])], cast=True)
            prev_sp = [(spS, spill_vals[-1])] if spill_vals else []
            _, vSP = tick(spS)
            em.op("sync",
                  lambda e, s=s: e.dma_start(x1R[:, :, 256 * s:256 * s + 256],
                                             x1sub[:, :, 0:256]),
                  waits=[(SV, x1_vals[15])] + dbg_gate() + prev_sp, inc=(spS, 16))
            spill_vals.append(vSP)

        attn_fin = {"ST": ST.n, "SV": SV.n, "SS": SS.n}
        astack.close()

        fstack = ExitStack()
        fent = fstack.enter_context
        x1h = fent(nc.sbuf_tensor("x1h", [128, 2, 16, 512], BF16))
        hT = fent(nc.sbuf_tensor("hT", [128, 64, 512], BF16))
        w1r = fent(nc.sbuf_tensor("w1r", [128, 2, 16, 128], BF16))
        w3r = fent(nc.sbuf_tensor("w3r", [128, 2, 16, 128], BF16))
        silt = fent(nc.sbuf_tensor("silt", [128, 2, 512], BF16))
        w2r = fent(nc.sbuf_tensor("w2r", [128, 5, 2, 2048], BF16))
        accT = fent(nc.sbuf_tensor("accT", [128, 16, 512], F32))

        barrier = [(ST, attn_fin["ST"]), (SV, attn_fin["SV"]), (SS, attn_fin["SS"])]
        hslots = [hT[:, j, :] for j in range(NH)] + [silt[:, 0, :]]

        x1Rf = X1SP.rearrange("(c p) (f t) -> p f c t", p=128, f=2)
        _, x1rd_v = tick(x1S)
        em.op("sync",
              lambda e: e.dma_start(x1h[:, :, :, :], x1Rf),
              waits=barrier + [(spS, spill_vals[-1])], inc=(x1S, 16))

        w1gate = [[], []]
        w3gate = [[], []]
        w2gate = [[] for _ in range(5)]
        f_w2_fin = 0
        f_out_val = 0
        for f in range(NSF):
            hp_vals = [0] * NH
            vhm_hist = [0] * NH
            vsil_hist = [0] * NH
            for m in range(NH):
                slot = m % 2
                b1 = m % 4
                b3 = 4 + (m % 4)
                _, vd1 = tick(w1S[slot])
                em.op("gpsimd",
                      lambda e, m=m, slot=slot: e.dma_start(
                          w1r[:, slot, :, :], W1R[:, :, 128 * m:128 * m + 128]),
                      waits=(barrier if (f == 0 and m == 0) else []) + list(w1gate[slot]),
                      inc=(w1S[slot], 16))
                _, vd3 = tick(w3S[slot])
                em.op("gpsimd",
                      lambda e, m=m, slot=slot: e.dma_start(
                          w3r[:, slot, :, :], W3R[:, :, 128 * m:128 * m + 128]),
                      waits=list(w3gate[slot]), inc=(w3S[slot], 16))
                s1 = s3 = 0
                for kc in range(16):
                    waits = []
                    if kc == 0:
                        waits = [(w1S[slot], vd1), (w3S[slot], vd3), (x1S, x1rd_v)] + \
                            list(bank_gate[b1]) + list(bank_gate[b3])
                        if f == 1 and m == 0:
                            waits.append((ST, f_w2_fin))
                            waits.append((outS, f_out_val))
                    inc = None
                    if kc == 15:
                        inc, s1 = tick(ST)
                    em.op("tensor",
                          lambda e, kc=kc, slot=slot, b1=b1, f=f: e.matmul(
                              ps[:, 512 * b1:512 * b1 + 512],
                              w1r[:, slot, kc, :], x1h[:, f, kc, :],
                              start=(kc == 0), stop=(kc == 15)),
                          waits=waits, inc=inc)
                    inc = None
                    if kc == 15:
                        inc, s3 = tick(ST)
                    em.op("tensor",
                          lambda e, kc=kc, slot=slot, b3=b3, f=f: e.matmul(
                              ps[:, 512 * b3:512 * b3 + 512],
                              w3r[:, slot, kc, :], x1h[:, f, kc, :],
                              start=(kc == 0), stop=(kc == 15)),
                          inc=inc)
                w1gate[slot] = [(ST, s1)]
                w3gate[slot] = [(ST, s3)]
                inc, vsil = tick(SS)
                em.op("scalar",
                      lambda e, m=m, b1=b1: e.activation(silt[:, m % 2, :],
                                                         ps[:, 512 * b1:512 * b1 + 512], AF.Silu),
                      waits=[(ST, s1)] + ([(SV, vhm_hist[m - 2])] if m >= 2 else []), inc=inc)
                inc, vhm = tick(SV)
                em.op("vector",
                      lambda e, m=m, b3=b3: e.tensor_mul(hT[:, m, :], silt[:, m % 2, :],
                                                         ps[:, 512 * b3:512 * b3 + 512]),
                      waits=[(ST, s3), (SS, vsil)] +
                            ([(ST, f_w2_fin)] if (f == 1 and m == 0) else []),
                      inc=inc)
                vhm_hist[m] = vhm
                bank_gate[b3] = [(SV, vhm)]
                vsil_hist[m] = vsil
                if KDBG == "hpre":
                    hp_vals[m] = 0
                    bank_gate[b1] = [(SV, vhm)]
                    continue
                def emit_tr(j):
                    bj = j % 4
                    inc, vtr = tick(ST)
                    em.op("tensor",
                          lambda e, j=j, bj=bj: e.matmul(ps[:, 512 * bj:512 * bj + 512],
                                                         h128b[:, :], hT[:, j, :],
                                                         start=True, stop=True),
                          waits=[(SV, vhm_hist[j]), (SS, vsil_hist[j]), (hadS, vHAD)], inc=inc)
                    inc, hp_vals[j] = tick(SS)
                    em.op("scalar",
                          lambda e, j=j, bj=bj: e.activation(hT[:, j, :],
                                                             ps[:, 512 * bj:512 * bj + 512],
                                                             AF.Copy, scale=ISQ),
                          waits=[(ST, vtr)], inc=inc)
                    bank_gate[bj] = [(SS, hp_vals[j])]
                if m >= 1:
                    emit_tr(m - 1)
                if m == NH - 1:
                    for _ in range(3):
                        em.op("vector", lambda e: e.tensor_add(w_rep[:, :], zeros[:, :],
                                                               zeros[:, :]))
                    inc, vsp = tick(SV)
                    em.op("vector", lambda e: e.tensor_add(w_rep[:, :], zeros[:, :],
                                                           zeros[:, :]), inc=inc)
                    vhm_hist[m] = max(vhm_hist[m], vsp)
                    emit_tr(m)
            dbg_dump("h", lambda: hT[:, 0:16, :], [(SS, max(hp_vals))], cast=True, n=512)
            dbg_dump("hpre", lambda: hT[:, 0:16, :], [(SV, vhm_hist[15])], cast=True, n=512)
            phys = list(range(NH))
            scratch = NH
            bf_first = [(SS, max(hp_vals))] + dbg_gate()
            bf_fin = 0
            for d in [1, 2, 4, 8, 16, 32]:
                for j in range(NH):
                    if j & d:
                        continue
                    a, b = phys[j], phys[j + d]
                    em.op("vector",
                          lambda e, a=a, b=b, sc=scratch: e.tensor_add(
                              hslots[sc], hslots[a], hslots[b]),
                          waits=bf_first)
                    bf_first = []
                    inc, bf_fin = tick(SV)
                    em.op("vector",
                          lambda e, a=a, b=b: e.tensor_sub(hslots[b], hslots[a], hslots[b]),
                          inc=inc)
                    phys[j] = scratch
                    scratch = a
            if KDBG == "hb" and not dbg_state["used"]:
                dbg_state["used"] = True
                _, vdg = tick(dbgS)
                em.op("gpsimd",
                      lambda e, phys=list(phys): e.dma_start(
                          dbg_state["ap"],
                          hT[:, 0, :].tensor[:, [phys[j] for j in range(16)], :]
                          if False else hT[:, 0:16, :]),
                      waits=[(SV, bf_fin)], inc=(dbgS, 16))
                dbg_state["gate"] = (dbgS, vdg)
            pair_retire = [0] * 32
            prev_evac = [0, 0]
            last_acc = [0] * 16
            gend = 0
            for g in range(8):
                pr_waits = []
                for p in range(4 * g, 4 * g + 4):
                    slot = p % 5
                    _, pr = tick(w2S[slot])
                    em.op("gpsimd",
                          lambda e, p=p, slot=slot: e.dma_start(
                              w2r[:, slot, :, :], W2R[:, 2 * p:2 * p + 2, :]),
                          waits=list(w2gate[slot]), inc=(w2S[slot], 16))
                    w2gate[slot] = []
                    pair_retire[p] = (w2S[slot], pr)
                g_stops = [0] * 16
                for mo in range(16):
                    bk = mo % 2
                    for jj in range(8):
                        j = 8 * g + jj
                        p = j // 2
                        waits = []
                        if jj == 0:
                            waits = [pair_retire[4 * g], pair_retire[4 * g + 1],
                                     pair_retire[4 * g + 2], pair_retire[4 * g + 3],
                                     (SV, bf_fin), (SV, prev_evac[bk])] + dbg_gate()
                            if g == 0 and mo < 2:
                                waits += bank_gate[bk]
                        inc = None
                        if jj == 7:
                            inc, g_stops[mo] = tick(ST)
                        em.op("tensor",
                              lambda e, jj=jj, mo=mo, j=j, p=p, bk=bk, ph=phys[j]: e.matmul(
                                  ps[:, 512 * bk:512 * bk + 512],
                                  w2r[:, p % 5, j % 2, 128 * mo:128 * mo + 128],
                                  hslots[ph],
                                  start=(jj == 0), stop=(jj == 7)),
                              waits=waits, inc=inc)
                    inc, v = tick(SV)
                    if g == 0:
                        em.op("vector",
                              lambda e, mo=mo, bk=bk: e.tensor_copy(
                                  accT[:, mo, :], ps[:, 512 * bk:512 * bk + 512]),
                              waits=[(ST, g_stops[mo])] +
                                    ([(outS, f_out_val)] if f == 1 and mo < 2 else []),
                              inc=inc)
                    else:
                        em.op("vector",
                              lambda e, mo=mo, bk=bk: e.tensor_add(
                                  accT[:, mo, :], accT[:, mo, :],
                                  ps[:, 512 * bk:512 * bk + 512]),
                              waits=[(ST, g_stops[mo])], inc=inc)
                    prev_evac[bk] = v
                    last_acc[mo] = v
                gend = g_stops[15]
                for p in range(4 * g, 4 * g + 4):
                    w2gate[p % 5] = [(ST, gend)]
            f_w2_fin = gend
            bank_gate[0] = [(SV, last_acc[14])]
            bank_gate[1] = [(SV, last_acc[15])]
            dbg_dump("acc", lambda: accT[:, :, :], [(SV, last_acc[15])], n=512)
            z2_vals = [0] * 16
            for mo in range(16):
                inc, z2_vals[mo] = tick(SV)
                em.op("vector",
                      lambda e, mo=mo, f=f: e.tensor_add(accT[:, mo, :], accT[:, mo, :],
                                                         x1h[:, f, mo, :]),
                      waits=[(SV, last_acc[mo])] + (dbg_gate() if mo == 0 else []), inc=inc)
            o2_vals = [0] * 16

            def ln2_writer(mu_b, rs_b, bc_ready, o2_vals=o2_vals):
                last = 0
                for mo in range(16):
                    em.op("vector",
                          lambda e, mo=mo, mu_b=mu_b: e.tensor_sub(
                              accT[:, mo, :], accT[:, mo, :], mu_b),
                          waits=[bc_ready] if mo == 0 else [])
                    inc, o2_vals[mo] = tick(SV)
                    em.op("vector",
                          lambda e, mo=mo, rs_b=rs_b: e.tensor_mul(
                              accT[:, mo, :], accT[:, mo, :], rs_b),
                          inc=inc)
                    last = o2_vals[mo]
                return last

            ln_phase(lambda mo: accT[:, mo, :], 512, (1024, 1536), (2048, 2560),
                     ln2_writer, [(SV, z2_vals[mo]) for mo in range(16)])
            _, f_out_val = tick(outS)
            em.op("sync",
                  lambda e, f=f: e.dma_start(outR[:, :, 512 * f:512 * f + 512], accT[:, :, :]),
                  waits=[(SV, o2_vals[15])], inc=(outS, 16))
        fstack.close()

        em.wait("sync", [(outS, outS.n), (spS, spS.n)])

        with nc.Block() as block:
            @block.sync
            def _(e):
                for f in em.streams["sync"]:
                    f(e)

            @block.gpsimd
            def _(e):
                for f in em.streams["gpsimd"]:
                    f(e)

            @block.tensor
            def _(e):
                for f in em.streams["tensor"]:
                    f(e)

            @block.vector
            def _(e):
                for f in em.streams["vector"]:
                    f(e)

            @block.scalar
            def _(e):
                for f in em.streams["scalar"]:
                    f(e)
    return nc


_NC_CACHE = None
LAST_RESULTS = None


def _get_nc():
    global _NC_CACHE
    if _NC_CACHE is None:
        _NC_CACHE = build_nc()
    return _NC_CACHE


def kernel(L0_attn_x, L0_attn_time_decay, L0_attn_time_first, L0_attn_tm_k, L0_attn_tm_v,
           L0_attn_tm_r, L0_attn_tm_g, L0_attn_Wr, L0_attn_Wk, L0_attn_Wv, L0_attn_Wg,
           L0_attn_Wo, L0_attn_ln_g, L0_attn_ln_b, L0_ffn_W1, L0_ffn_W2, L0_ffn_W3):
    f32 = np.float32
    x = np.asarray(L0_attn_x, f32)[0]
    xTt = np.ascontiguousarray(x.T)

    def vt(v):
        return np.ascontiguousarray(np.asarray(v, f32).reshape(16, 128).T)

    PARh = np.concatenate([vt(L0_attn_tm_k), vt(L0_attn_tm_v), vt(L0_attn_tm_r),
                           vt(L0_attn_tm_g), vt(L0_attn_time_decay), vt(L0_attn_time_first),
                           vt(L0_attn_ln_g), vt(L0_attn_ln_b)], axis=1)
    Hm = np.array([[1.0]], f32)
    while Hm.shape[0] < 128:
        Hm = np.block([[Hm, Hm], [Hm, -Hm]])
    Hm = np.ascontiguousarray(Hm.astype(f32))

    WrT = np.ascontiguousarray(np.asarray(L0_attn_Wr, f32).T)
    WkT = np.ascontiguousarray(np.asarray(L0_attn_Wk, f32).T)
    WvT = np.ascontiguousarray(np.asarray(L0_attn_Wv, f32).T)
    WgT = np.ascontiguousarray(np.asarray(L0_attn_Wg, f32).T)
    WoT = np.ascontiguousarray(np.asarray(L0_attn_Wo, f32).T)
    W1T = np.ascontiguousarray(np.asarray(L0_ffn_W1, f32).T)
    W3T = np.ascontiguousarray(np.asarray(L0_ffn_W3, f32).T)
    W2T = np.ascontiguousarray(np.asarray(L0_ffn_W2, f32).T)

    in_maps = []
    for c in range(NCORES):
        s0 = TPC * c
        if c == 0:
            sl = np.concatenate([np.zeros((C, WARM), f32), xTt[:, 1:2], xTt[:, 0:TPC]], axis=1)
            wm = np.zeros((128, 1), f32)
        else:
            sl = np.ascontiguousarray(xTt[:, s0 - WARM - 1: s0 + TPC])
            wm = np.ones((128, 1), f32)
        in_maps.append({
            "xT": sl, "WrT": WrT, "WkT": WkT, "WvT": WvT, "WgT": WgT, "WoT": WoT,
            "W1T": W1T, "W3T": W3T, "W2T": W2T,
            "PAR": np.ascontiguousarray(np.concatenate([PARh, wm], axis=1)),
            "HAD": Hm,
        })

    nc = _get_nc()
    res = run_bass_kernel_spmd(nc, in_maps, core_ids=list(range(NCORES)))
    global LAST_RESULTS
    LAST_RESULTS = res.results
    outs = [np.asarray(r["OUT"]) for r in res.results]
    full = np.concatenate(outs, axis=1)
    return np.ascontiguousarray(full.T)[None].astype(np.float32)


# version 04
# speedup vs baseline: 1.0992x; 1.0992x over previous
import numpy as np
import concourse.bass as bass
import concourse.mybir as mybir
from concourse.bass_utils import run_bass_kernel_spmd
from contextlib import ExitStack

F32 = mybir.dt.float32
BF16 = mybir.dt.bfloat16
AF = mybir.ActivationFunctionType
OP = mybir.AluOpType

C = 2048
T = 8192
NCORES = 8
TPC = T // NCORES
KC = 16
TA = 256
NSA = TPC // TA
WARM = 128
TF = 512
NSF = TPC // TF
H = 8192
NH = H // 128
EPS = 1e-5
ISQ = float(1.0 / np.sqrt(8192.0))
XCOLS = 1 + WARM + TPC


class Sem:
    def __init__(self, h, name, step=1):
        self.h, self.name, self.n, self.step = h, name, 0, step

    def tick(self):
        self.n += self.step
        return self.n


class Em:
    ENGS = ("sync", "gpsimd", "tensor", "vector", "scalar")

    def __init__(self):
        self.streams = {k: [] for k in self.ENGS}
        self.wm = {k: {} for k in self.ENGS}

    def op(self, eng, fn, waits=(), inc=None):
        wl = []
        wm = self.wm[eng]
        for sw in waits:
            if sw is None:
                continue
            sem, val = sw
            if val is None or val <= 0:
                continue
            if val > wm.get(sem.name, 0):
                wm[sem.name] = val
                wl.append((sem, val))

        def run(e, fn=fn, wl=wl, inc=inc):
            for sem, val in wl:
                e.wait_ge(sem.h, val)
            ins = fn(e)
            if inc is not None:
                sem, k = inc
                ins.then_inc(sem.h, k)
        self.streams[eng].append(run)

    def wait(self, eng, waits):
        wl = []
        wm = self.wm[eng]
        for sem, val in waits:
            if val and val > wm.get(sem.name, 0):
                wm[sem.name] = val
                wl.append((sem, val))
        if wl:
            def run(e, wl=wl):
                for sem, val in wl:
                    e.wait_ge(sem.h, val)
            self.streams[eng].append(run)


import os
KDBG = os.environ.get("KDBG", "")
KBFBATCH = bool(os.environ.get("KBFBATCH", ""))


def build_nc():
    nc = bass.Bass("TRN2", target_bir_lowering=False)
    xT = nc.dram_tensor("xT", [C, XCOLS], F32, kind="ExternalInput").ap()
    WrT = nc.dram_tensor("WrT", [C, C], F32, kind="ExternalInput").ap()
    WkT = nc.dram_tensor("WkT", [C, C], F32, kind="ExternalInput").ap()
    WvT = nc.dram_tensor("WvT", [C, C], F32, kind="ExternalInput").ap()
    WgT = nc.dram_tensor("WgT", [C, C], F32, kind="ExternalInput").ap()
    WoT = nc.dram_tensor("WoT", [C, C], F32, kind="ExternalInput").ap()
    W13 = nc.dram_tensor("W13", [C, 64, 256], F32, kind="ExternalInput").ap()
    W2T = nc.dram_tensor("W2T", [H, C], F32, kind="ExternalInput").ap()
    PAR = nc.dram_tensor("PAR", [128, 129], F32, kind="ExternalInput").ap()
    HAD = nc.dram_tensor("HAD", [128, 128], F32, kind="ExternalInput").ap()
    OUT = nc.dram_tensor("OUT", [C, TPC], F32, kind="ExternalOutput").ap()
    X1SP = nc.dram_tensor("X1SP", [C, TPC], BF16, kind="Internal").ap()

    xTr = xT.rearrange("(c p) t -> p c t", p=128)
    outR = OUT.rearrange("(c p) t -> p c t", p=128)
    x1R = X1SP.rearrange("(c p) t -> p c t", p=128)
    W13R = W13.rearrange("(c p) m n -> p c m n", p=128)
    W2R = W2T.rearrange("(c p) n -> p c n", p=128)

    em = Em()
    dbg_state = {"ap": None, "gate": None, "used": False}
    if KDBG:
        dbg_state["ap"] = nc.dram_tensor("DBG", [128, 16, 512], F32, kind="ExternalOutput").ap()
    with ExitStack() as stack:
        ent = stack.enter_context
        def dsem(name):
            return Sem(ent(nc.semaphore(name)), name, 16)
        wringS = [dsem("A%d" % i) for i in range(4)]
        w1S = [dsem("B%d" % i) for i in range(3)]
        w2S = [dsem("C%d" % i) for i in range(4)]
        xtS = dsem("XT")
        xfS = dsem("XF")
        x1S = dsem("X1")
        spS = dsem("SP")
        outS = dsem("OU")
        parS = dsem("PA")
        hadS = dsem("HA")
        dbgS = dsem("DB")
        ST = Sem(ent(nc.semaphore("ST")), "ST")
        SV = Sem(ent(nc.semaphore("SV")), "SV")
        SS = Sem(ent(nc.semaphore("SS")), "SS")

        def tick(sem):
            v = sem.tick()
            return (sem, sem.step), v

        def dbg_dump(tag, src_fn, waits, cast=False, n=256):
            if KDBG != tag or dbg_state["used"]:
                return
            dbg_state["used"] = True
            _, v = tick(dbgS)
            em.op("gpsimd" if cast else "sync",
                  lambda e: e.dma_start(dbg_state["ap"][:, :, 0:n], src_fn()),
                  waits=waits, inc=(dbgS, 16))
            dbg_state["gate"] = (dbgS, v)

        def dbg_gate():
            g = dbg_state["gate"]
            return [g] if g is not None else []

        ps = ent(nc.psum_tensor("ps", [128, 4096], F32))
        h128b = ent(nc.sbuf_tensor("h128b", [128, 128], BF16))
        par = ent(nc.sbuf_tensor("par", [128, 129], F32))
        cns = ent(nc.sbuf_tensor("cns", [128, 48], F32))
        zeros = ent(nc.sbuf_tensor("zeros", [128, 256], F32))
        w_rep = ent(nc.sbuf_tensor("w_rep", [128, 256], F32))
        carA = ent(nc.sbuf_tensor("carA", [128, 16], F32))
        carB = ent(nc.sbuf_tensor("carB", [128, 16], F32))
        statf = ent(nc.sbuf_tensor("statf", [1, 4, 512], F32))
        statb = ent(nc.sbuf_tensor("statb", [1, 2, 512], BF16))
        lnrA = ent(nc.sbuf_tensor("lnrA", [128, 2, 512], BF16))
        lnrB = ent(nc.sbuf_tensor("lnrB", [128, 2, 512], BF16))

        tm = {k: par[:, 16 * i:16 * (i + 1)] for i, k in
              enumerate(["k", "v", "r", "g", "td", "tf", "lng", "lnb"])}
        w_c = cns[:, 0:16]
        eu_c = cns[:, 16:32]
        c2_c = cns[:, 32:48]

        wmsk = par[:, 128:129]
        _, vPAR = tick(parS)
        em.op("sync", lambda e: e.dma_start(par[:, :], PAR), inc=(parS, 16))
        vWM = vPAR
        _, vHAD = tick(hadS)
        em.op("gpsimd", lambda e: e.dma_start(h128b[:, :], HAD), inc=(hadS, 16))
        ones_col = h128b[:, 0:1]
        ones_row = h128b[0:1, :]

        em.op("scalar", lambda e: e.activation(cns[:, 0:16], tm["td"], AF.Exp),
              waits=[(parS, vPAR)])
        em.op("scalar", lambda e: e.activation(cns[:, 0:16], cns[:, 0:16], AF.Exp, scale=-1.0))
        inc, vEU = tick(SS)
        em.op("scalar", lambda e: e.activation(cns[:, 16:32], tm["tf"], AF.Exp), inc=inc)
        em.op("vector", lambda e: e.tensor_mul(cns[:, 32:48], cns[:, 0:16], cns[:, 16:32]),
              waits=[(SS, vEU)])
        inc, vZ = tick(SV)
        em.op("vector", lambda e: e.memset(zeros[:, :], 0.0), inc=inc)
        em.op("vector", lambda e: e.memset(w_rep[:, :], 0.0))
        inc, vC2 = tick(SV)
        em.op("vector", lambda e: e.tensor_scalar(cns[:, 32:48], cns[:, 32:48], -1.0, 1.0,
                                                  OP.mult, OP.add), inc=inc)

        bank_gate = [[] for _ in range(8)]
        last_ln_roll = [0]

        def ln_phase(src_fn, N, st_cols, bc_cols, out_writer, src_ready):
            cp_vals = [0] * 16
            rfv = [0] * 16
            for mo in range(16):
                w = [src_ready[mo]] if src_ready else []
                if mo >= 2:
                    w.append((ST, rfv[mo - 2]))
                else:
                    w.append((ST, last_ln_roll[0]))
                em.op("scalar",
                      lambda e, mo=mo: e.activation(lnrA[:, mo % 2, 0:N], src_fn(mo), AF.Copy),
                      waits=w)
                inc, cp_vals[mo] = tick(SS)
                em.op("scalar",
                      lambda e, mo=mo: e.activation(lnrB[:, mo % 2, 0:N], src_fn(mo), AF.Square),
                      inc=inc)
            for mo in range(16):
                waits = [(SS, cp_vals[mo])]
                if mo == 0:
                    waits += bank_gate[st_cols[0] // 512] + bank_gate[st_cols[1] // 512]
                sb_same = (st_cols[0] // 512) == (st_cols[1] // 512)
                em.op("tensor",
                      lambda e, mo=mo: e.matmul(ps[0:1, st_cols[0]:st_cols[0] + N], ones_col,
                                                lnrA[:, mo % 2, 0:N],
                                                start=(mo == 0), stop=(mo == 15)),
                      waits=waits)
                inc, rfv[mo] = tick(ST)
                em.op("tensor",
                      lambda e, mo=mo, sb_same=sb_same: e.matmul(
                          ps[0:1, st_cols[1]:st_cols[1] + N], ones_col,
                          lnrB[:, mo % 2, 0:N],
                          start=(mo == 0 and not sb_same), stop=(mo == 15)),
                      inc=inc)
            last_ln_roll[0] = rfv[15]
            st_stop = rfv[15]
            em.op("vector",
                  lambda e: e.tensor_scalar(statf[0:1, 0, 0:N], ps[0:1, st_cols[0]:st_cols[0] + N],
                                            1.0 / 2048.0, None, OP.mult),
                  waits=[(ST, st_stop)])
            em.op("vector",
                  lambda e: e.tensor_scalar(statf[0:1, 1, 0:N], ps[0:1, st_cols[1]:st_cols[1] + N],
                                            1.0 / 2048.0, EPS, OP.mult, OP.add))
            em.op("vector", lambda e: e.tensor_mul(statf[0:1, 2, 0:N], statf[0:1, 0, 0:N],
                                                   statf[0:1, 0, 0:N]))
            inc, vmunge = tick(SV)
            em.op("vector", lambda e: e.tensor_sub(statf[0:1, 1, 0:N], statf[0:1, 1, 0:N],
                                                   statf[0:1, 2, 0:N]), inc=inc)
            bank_gate[st_cols[0] // 512] = [(SV, vmunge)]
            bank_gate[st_cols[1] // 512] = [(SV, vmunge)]
            em.op("scalar",
                  lambda e: e.activation(statf[0:1, 3, 0:N], statf[0:1, 1, 0:N], AF.Ln),
                  waits=[(SV, vmunge)])
            em.op("scalar", lambda e: e.activation(statf[0:1, 3, 0:N], statf[0:1, 3, 0:N],
                                                   AF.Exp, scale=-0.5))
            em.op("scalar", lambda e: e.activation(statb[0:1, 0, 0:N], statf[0:1, 0, 0:N], AF.Copy))
            inc, vcast = tick(SS)
            em.op("scalar", lambda e: e.activation(statb[0:1, 1, 0:N], statf[0:1, 3, 0:N], AF.Copy),
                  inc=inc)
            em.op("tensor",
                  lambda e: e.matmul(ps[:, bc_cols[0]:bc_cols[0] + N], ones_row, statb[0:1, 0, 0:N],
                                     start=True, stop=True),
                  waits=[(SS, vcast), (hadS, vHAD)] + bank_gate[bc_cols[0] // 512])
            inc, vbc = tick(ST)
            bc_same = (bc_cols[0] // 512) == (bc_cols[1] // 512)
            em.op("tensor",
                  lambda e, bc_same=bc_same: e.matmul(
                      ps[:, bc_cols[1]:bc_cols[1] + N], ones_row, statb[0:1, 1, 0:N],
                      start=(not bc_same), stop=True),
                  waits=bank_gate[bc_cols[1] // 512], inc=inc)
            mu_b = ps[:, bc_cols[0]:bc_cols[0] + N]
            rs_b = ps[:, bc_cols[1]:bc_cols[1] + N]
            last = out_writer(mu_b, rs_b, (ST, vbc))
            bank_gate[bc_cols[0] // 512] = [(SV, last)]
            bank_gate[bc_cols[1] // 512] = [(SV, last)]

        astack = ExitStack()
        aent = astack.enter_context
        xtile = aent(nc.sbuf_tensor("xtile", [128, 16, 385], BF16))
        xf32 = aent(nc.sbuf_tensor("xf32", [128, 16, 256], F32))
        xx_r = aent(nc.sbuf_tensor("xx_r", [128, 256], F32))
        stg_k = aent(nc.sbuf_tensor("stg_k", [128, 16, 256], BF16))
        stg_v = aent(nc.sbuf_tensor("stg_v", [128, 16, 256], BF16))
        stg_r = aent(nc.sbuf_tensor("stg_r", [128, 16, 256], BF16))
        stg_g = aent(nc.sbuf_tensor("stg_g", [128, 16, 256], BF16))
        ekT = aent(nc.sbuf_tensor("ekT", [128, 16, 256], F32))
        uT = aent(nc.sbuf_tensor("uT", [128, 16, 256], F32))
        apg = aent(nc.sbuf_tensor("apg", [128, 2, 257], F32))
        bpg = aent(nc.sbuf_tensor("bpg", [128, 2, 257], F32))
        numb = aent(nc.sbuf_tensor("numb", [128, 2, 256], F32))
        denb = aent(nc.sbuf_tensor("denb", [128, 2, 256], F32))
        invb = aent(nc.sbuf_tensor("invb", [128, 2, 256], F32))
        rsig = aent(nc.sbuf_tensor("rsig", [128, 16, 256], BF16))
        gsr = aent(nc.sbuf_tensor("gsr", [128, 2, 256], BF16))
        wring = aent(nc.sbuf_tensor("wring", [128, 4, 2, 2048], BF16))
        attn_b16 = stg_r
        x1sub = stg_g

        wring_gate = [[] for _ in range(4)]
        spill_vals = []

        def proj(wdram, rhs_fn, N, rhs_ready):
            wR = wdram.rearrange("(c p) n -> p c n", p=128)
            dv = [0] * 8

            def emit_dma(p, waits):
                slot = p % 4
                _, dv[p] = tick(wringS[slot])
                em.op("gpsimd",
                      lambda e, p=p, slot=slot, wR=wR: e.dma_start(
                          wring[:, slot, :, :], wR[:, 2 * p:2 * p + 2, :]),
                      waits=waits, inc=(wringS[slot], 16))

            for p in range(4):
                emit_dma(p, list(wring_gate[p]))
                wring_gate[p] = []
            stop_vals = [0] * 16
            kc_tick = [0] * 16
            for kc in range(16):
                p, i = kc // 2, kc % 2
                slot = p % 4
                for mo in range(16):
                    waits = []
                    if mo == 0:
                        waits.append((wringS[slot], dv[p]))
                        if kc == 0:
                            waits += list(rhs_ready)
                    if kc == 0:
                        waits += bank_gate[mo // 2]
                    inc = None
                    if kc == 15:
                        inc, stop_vals[mo] = tick(ST)
                    elif kc % 2 == 1 and mo == 15:
                        inc, kc_tick[kc] = tick(ST)
                    em.op("tensor",
                          lambda e, kc=kc, i=i, slot=slot, mo=mo, N=N, rhs_fn=rhs_fn: e.matmul(
                              ps[:, 256 * mo:256 * mo + N],
                              wring[:, slot, i, 128 * mo:128 * mo + 128],
                              rhs_fn(kc),
                              start=(kc == 0 and mo % 2 == 0), stop=(kc == 15)),
                          waits=waits, inc=inc)
                if kc in (1, 3, 5, 7):
                    emit_dma(4 + (kc - 1) // 2, [(ST, kc_tick[kc])])
            for b in range(8):
                bank_gate[b] = []
            wring_gate[0] = [(ST, kc_tick[9])]
            wring_gate[1] = [(ST, kc_tick[11])]
            wring_gate[2] = [(ST, kc_tick[13])]
            wring_gate[3] = [(ST, stop_vals[15])]
            return stop_vals

        def lerp_phase(base, n, sigs, xt_ready, extra=()):
            fin = 0
            for kc in range(16):
                w = list(extra) + [xt_ready] if kc == 0 else []
                em.op("vector",
                      lambda e, kc=kc, base=base, n=n: e.tensor_sub(
                          xx_r[:, 0:n], xtile[:, kc, base:base + n],
                          xtile[:, kc, base + 1:base + 1 + n]),
                      waits=w)
                for name, dst in sigs:
                    inc, fin = tick(SV)
                    em.op("vector",
                          lambda e, kc=kc, base=base, n=n, name=name, dst=dst:
                          e.scalar_tensor_tensor(
                              dst[:, kc, 0:n], xx_r[:, 0:n], tm[name][:, kc:kc + 1],
                              xtile[:, kc, base + 1:base + 1 + n], OP.mult, OP.add),
                          inc=inc)
            return fin

        _, vXT = tick(xtS)
        em.op("gpsimd", lambda e: e.dma_start(xtile[:, :, 0:385], xTr[:, :, 0:385]),
              inc=(xtS, 16))
        xt_ready = (xtS, vXT)

        lerp_fin = lerp_phase(0, 128, [("k", stg_k), ("v", stg_v)], xt_ready)
        kstops = proj(WkT, lambda kc: stg_k[:, kc, 0:128], 128, [(SV, lerp_fin)])
        ek_vals = [0] * 16
        for mo in range(16):
            inc, ek_vals[mo] = tick(SS)
            em.op("scalar",
                  lambda e, mo=mo: e.activation(ekT[:, mo, 0:128],
                                                ps[:, 256 * mo:256 * mo + 128], AF.Exp),
                  waits=[(ST, kstops[mo | 1])], inc=inc)
            bank_gate[mo // 2] = [(SS, ek_vals[mo])]
        vstops = proj(WvT, lambda kc: stg_v[:, kc, 0:128], 128, [(SV, lerp_fin)])
        u_vals = [0] * 16
        for mo in range(16):
            inc, u_vals[mo] = tick(SV)
            em.op("vector",
                  lambda e, mo=mo: e.tensor_mul(uT[:, mo, 0:128],
                                                ps[:, 256 * mo:256 * mo + 128],
                                                ekT[:, mo, 0:128]),
                  waits=[(ST, vstops[mo | 1]), (SS, ek_vals[mo])], inc=inc)
            bank_gate[mo // 2] = [(SV, u_vals[mo])]
        def pro_stage(mo):
            g = mo % 2
            em.op("vector",
                  lambda e, mo=mo: e.scalar_tensor_tensor(
                      w_rep[:, 0:128], zeros[:, 0:128], w_c[:, mo:mo + 1],
                      zeros[:, 0:128], OP.add, OP.add),
                  waits=[(SV, u_vals[mo]), (SV, vZ), (SV, vC2)])
            em.op("vector",
                  lambda e, mo=mo, g=g: e.tensor_tensor_scan(
                      apg[:, g, 0:128], w_rep[:, 0:128], uT[:, mo, 0:128], 0.0,
                      OP.mult, OP.add))
            em.op("vector",
                  lambda e, mo=mo, g=g: e.tensor_tensor_scan(
                      bpg[:, g, 0:128], w_rep[:, 0:128], ekT[:, mo, 0:128], 0.0,
                      OP.mult, OP.add))

        def pro_consume(mo):
            g = mo % 2
            em.op("vector", lambda e, mo=mo, g=g: e.tensor_copy(carA[:, mo:mo + 1],
                                                                apg[:, g, 127:128]))
            em.op("vector", lambda e, mo=mo, g=g: e.tensor_copy(carB[:, mo:mo + 1],
                                                                bpg[:, g, 127:128]))

        pro_stage(0)
        for mo in range(1, 16):
            pro_stage(mo)
            pro_consume(mo - 1)
        pro_consume(15)
        em.op("vector", lambda e: e.tensor_scalar(carA[:, :], carA[:, :], wmsk[:, 0:1],
                                                  None, OP.mult),
              waits=[(parS, vWM)])
        inc, vCAR = tick(SV)
        em.op("vector", lambda e: e.tensor_scalar(carB[:, :], carB[:, :], wmsk[:, 0:1],
                                                  None, OP.mult), inc=inc)

        prev_z_fin = 0
        prev_attnmul_fin = vCAR
        prev_uT_guard = vCAR
        prev_lerp_fin = lerp_fin
        pro_stop = vstops[15]
        for s in range(NSA):
            base = 128 if s == 0 else 0
            if s > 0:
                _, vXT = tick(xtS)
                em.op("gpsimd",
                      lambda e, s=s: e.dma_start(xtile[:, :, 0:257],
                                                 xTr[:, :, 128 + 256 * s:385 + 256 * s]),
                      waits=[(SV, prev_lerp_fin)], inc=(xtS, 16))
                xt_ready = (xtS, vXT)
            _, vXF = tick(xfS)
            em.op("sync",
                  lambda e, s=s: e.dma_start(xf32[:, :, :],
                                             xTr[:, :, 129 + 256 * s:385 + 256 * s]),
                  waits=[(SV, prev_z_fin)], inc=(xfS, 16))
            extra = [(ST, pro_stop)] if s == 0 else [(spS, spill_vals[-1])]
            lerp_fin = lerp_phase(base, 256,
                                  [("k", stg_k), ("v", stg_v), ("r", stg_r), ("g", stg_g)],
                                  xt_ready, extra=extra)
            prev_lerp_fin = lerp_fin
            kstops = proj(WkT, lambda kc: stg_k[:, kc, 0:256], 256, [(SV, lerp_fin)])
            ek_vals = [0] * 16
            for mo in range(16):
                inc, ek_vals[mo] = tick(SS)
                em.op("scalar",
                      lambda e, mo=mo: e.activation(ekT[:, mo, 0:256],
                                                    ps[:, 256 * mo:256 * mo + 256], AF.Exp),
                      waits=[(ST, kstops[mo | 1]), (SV, prev_attnmul_fin)], inc=inc)
                bank_gate[mo // 2] = [(SS, ek_vals[mo])]
            vstops = proj(WvT, lambda kc: stg_v[:, kc, 0:256], 256, [(SV, lerp_fin)])
            u_vals = [0] * 16
            for mo in range(16):
                inc, u_vals[mo] = tick(SV)
                em.op("vector",
                      lambda e, mo=mo: e.tensor_mul(uT[:, mo, 0:256],
                                                    ps[:, 256 * mo:256 * mo + 256],
                                                    ekT[:, mo, 0:256]),
                      waits=[(ST, vstops[mo | 1]), (SS, ek_vals[mo]), (SV, prev_uT_guard)],
                      inc=inc)
                bank_gate[mo // 2] = [(SV, u_vals[mo])]
            dbg_dump("stgk", lambda: stg_k[:, :, 0:256].rearrange("p c t -> p c t"),
                     [(SV, lerp_fin)], cast=True)
            dbg_dump("ek", lambda: ekT[:, :, :], [(SS, ek_vals[15])])
            dbg_dump("u", lambda: uT[:, :, :], [(SV, u_vals[15])])
            wkv_vals = [0] * 16
            vinv_hist = [0] * 16

            def sc_stage(mo, s=s):
                g = mo % 2
                em.op("vector",
                      lambda e, mo=mo: e.scalar_tensor_tensor(
                          w_rep[:, 0:256], zeros[:, 0:256], w_c[:, mo:mo + 1],
                          zeros[:, 0:256], OP.add, OP.add),
                      waits=[(SV, u_vals[mo])] + ([(SV, vCAR)] if (s == 0 and mo == 0) else []))
                em.op("vector", lambda e, mo=mo, g=g: e.tensor_copy(apg[:, g, 0:1],
                                                                    carA[:, mo:mo + 1]))
                em.op("vector",
                      lambda e, mo=mo, g=g: e.tensor_tensor_scan(
                          apg[:, g, 1:257], w_rep[:, 0:256], uT[:, mo, 0:256],
                          carA[:, mo:mo + 1], OP.mult, OP.add))
                em.op("vector", lambda e, mo=mo, g=g: e.tensor_copy(bpg[:, g, 0:1],
                                                                    carB[:, mo:mo + 1]))
                em.op("vector",
                      lambda e, mo=mo, g=g: e.tensor_tensor_scan(
                          bpg[:, g, 1:257], w_rep[:, 0:256], ekT[:, mo, 0:256],
                          carB[:, mo:mo + 1], OP.mult, OP.add))

            def sc_consume(mo):
                g = mo % 2
                em.op("vector", lambda e, mo=mo, g=g: e.tensor_copy(carA[:, mo:mo + 1],
                                                                    apg[:, g, 256:257]))
                em.op("vector", lambda e, mo=mo, g=g: e.tensor_copy(carB[:, mo:mo + 1],
                                                                    bpg[:, g, 256:257]))
                em.op("vector",
                      lambda e, mo=mo, g=g: e.tensor_scalar(
                          numb[:, g, :], apg[:, g, 1:257], eu_c[:, mo:mo + 1], None, OP.mult))
                em.op("vector",
                      lambda e, mo=mo, g=g: e.scalar_tensor_tensor(
                          numb[:, g, :], apg[:, g, 0:256], c2_c[:, mo:mo + 1],
                          numb[:, g, :], OP.mult, OP.add))
                em.op("vector",
                      lambda e, mo=mo, g=g: e.tensor_scalar(
                          denb[:, g, :], bpg[:, g, 1:257], eu_c[:, mo:mo + 1], None, OP.mult),
                      waits=([(SS, vinv_hist[mo - 2])] if mo >= 2 else []))
                inc, vden = tick(SV)
                em.op("vector",
                      lambda e, mo=mo, g=g: e.scalar_tensor_tensor(
                          denb[:, g, :], bpg[:, g, 0:256], c2_c[:, mo:mo + 1],
                          denb[:, g, :], OP.mult, OP.add), inc=inc)
                em.op("scalar",
                      lambda e, mo=mo, g=g: e.activation(invb[:, g, :], denb[:, g, :], AF.Ln),
                      waits=[(SV, vden)])
                inc, vinv = tick(SS)
                vinv_hist[mo] = vinv
                em.op("scalar",
                      lambda e, mo=mo, g=g: e.activation(invb[:, g, :], invb[:, g, :],
                                                         AF.Exp, scale=-1.0), inc=inc)
                inc, wkv_vals[mo] = tick(SV)
                em.op("vector",
                      lambda e, mo=mo, g=g: e.tensor_mul(ekT[:, mo, 0:256], numb[:, g, :],
                                                         invb[:, g, :]),
                      waits=[(SS, vinv)] + (dbg_gate() if mo == 0 else []), inc=inc)

            sc_stage(0)
            for mo in range(1, 16):
                sc_stage(mo)
                sc_consume(mo - 1)
            sc_consume(15)
            dbg_dump("wkv", lambda: ekT[:, :, :], [(SV, wkv_vals[15])])
            rstops = proj(WrT, lambda kc: stg_r[:, kc, 0:256], 256, [(SV, lerp_fin)])
            rs_vals = [0] * 16
            for mo in range(16):
                inc, rs_vals[mo] = tick(SS)
                em.op("scalar",
                      lambda e, mo=mo: e.activation(rsig[:, mo, 0:256],
                                                    ps[:, 256 * mo:256 * mo + 256], AF.Sigmoid),
                      waits=[(ST, rstops[mo | 1])], inc=inc)
                bank_gate[mo // 2] = [(SS, rs_vals[mo])]
            am_vals = [0] * 16
            for mo in range(16):
                inc, am_vals[mo] = tick(SV)
                em.op("vector",
                      lambda e, mo=mo: e.tensor_mul(ekT[:, mo, 0:256], ekT[:, mo, 0:256],
                                                    rsig[:, mo, 0:256]),
                      waits=[(SS, rs_vals[mo]), (SV, wkv_vals[mo])] +
                            (dbg_gate() if mo == 0 else []), inc=inc)
            gstops = proj(WgT, lambda kc: stg_g[:, kc, 0:256], 256, [(SV, lerp_fin)])
            ab_vals = [0] * 16
            for mo in range(16):
                inc, vgs = tick(SS)
                em.op("scalar",
                      lambda e, mo=mo: e.activation(gsr[:, mo % 2, :],
                                                    ps[:, 256 * mo:256 * mo + 256], AF.Sigmoid),
                      waits=[(ST, gstops[mo | 1])] +
                            ([(SV, ab_vals[mo - 2])] if mo >= 2 else []), inc=inc)
                bank_gate[mo // 2] = [(SS, vgs)]
                inc, ab_vals[mo] = tick(SV)
                em.op("vector",
                      lambda e, mo=mo: e.tensor_mul(attn_b16[:, mo, 0:256], ekT[:, mo, 0:256],
                                                    gsr[:, mo % 2, :]),
                      waits=[(SS, vgs), (SV, am_vals[mo])], inc=inc)
            prev_attnmul_fin = ab_vals[15]
            dbg_dump("attnin", lambda: attn_b16[:, :, 0:256], [(SV, ab_vals[15])], cast=True)
            ostops = proj(WoT, lambda kc: attn_b16[:, kc, 0:256], 256, [(SV, ab_vals[15])])
            o_vals = [0] * 16
            for mo in range(16):
                inc, o_vals[mo] = tick(SS)
                em.op("scalar",
                      lambda e, mo=mo: e.activation(uT[:, mo, 0:256],
                                                    ps[:, 256 * mo:256 * mo + 256], AF.Copy),
                      waits=[(ST, ostops[mo | 1])], inc=inc)
                bank_gate[mo // 2] = [(SS, o_vals[mo])]
            dbg_dump("o", lambda: uT[:, :, :], [(SS, o_vals[15])])
            z_vals = [0] * 16

            def attn_writer(mu_b, rs_b, bc_ready, vXF=vXF, z_vals=z_vals):
                last = 0
                for mo in range(16):
                    em.op("vector",
                          lambda e, mo=mo, mu_b=mu_b: e.tensor_sub(
                              numb[:, mo % 2, :], uT[:, mo, 0:256], mu_b),
                          waits=[bc_ready] if mo == 0 else [])
                    em.op("vector",
                          lambda e, mo=mo, rs_b=rs_b: e.tensor_mul(
                              numb[:, mo % 2, :], numb[:, mo % 2, :], rs_b))
                    em.op("vector",
                          lambda e, mo=mo: e.tensor_scalar(
                              numb[:, mo % 2, :], numb[:, mo % 2, :],
                              tm["lng"][:, mo:mo + 1], tm["lnb"][:, mo:mo + 1],
                              OP.mult, OP.add))
                    inc, z_vals[mo] = tick(SV)
                    em.op("vector",
                          lambda e, mo=mo: e.tensor_add(uT[:, mo, 0:256], numb[:, mo % 2, :],
                                                        xf32[:, mo, :]),
                          waits=[(xfS, vXF)] + (dbg_gate() if mo == 0 else []), inc=inc)
                    last = z_vals[mo]
                return last

            ln_phase(lambda mo: uT[:, mo, 0:256], 256, (0, 256), (512, 768),
                     attn_writer, [(SS, o_vals[mo]) for mo in range(16)])
            prev_z_fin = z_vals[15]
            x1_vals = [0] * 16

            def ln1_writer(mu_b, rs_b, bc_ready, x1_vals=x1_vals, gstop=gstops[15]):
                last = 0
                for mo in range(16):
                    em.op("vector",
                          lambda e, mo=mo, mu_b=mu_b: e.tensor_sub(
                              numb[:, mo % 2, :], uT[:, mo, 0:256], mu_b),
                          waits=[bc_ready] if mo == 0 else [])
                    inc, x1_vals[mo] = tick(SV)
                    em.op("vector",
                          lambda e, mo=mo, rs_b=rs_b: e.tensor_mul(
                              x1sub[:, mo, 0:256], numb[:, mo % 2, :], rs_b),
                          waits=[(ST, gstop)], inc=inc)
                    last = x1_vals[mo]
                return last

            dbg_dump("z", lambda: uT[:, :, :], [(SV, z_vals[15])])
            ln_phase(lambda mo: uT[:, mo, 0:256], 256, (1024, 1280), (1536, 1792),
                     ln1_writer, [(SV, z_vals[mo]) for mo in range(16)])
            prev_uT_guard = x1_vals[15]
            dbg_dump("x1", lambda: x1sub[:, :, 0:256], [(SV, x1_vals[15])], cast=True)
            prev_sp = [(spS, spill_vals[-1])] if spill_vals else []
            _, vSP = tick(spS)
            em.op("sync",
                  lambda e, s=s: e.dma_start(x1R[:, :, 256 * s:256 * s + 256],
                                             x1sub[:, :, 0:256]),
                  waits=[(SV, x1_vals[15])] + dbg_gate() + prev_sp, inc=(spS, 16))
            spill_vals.append(vSP)

        attn_fin = {"ST": ST.n, "SV": SV.n, "SS": SS.n}
        astack.close()

        fstack = ExitStack()
        fent = fstack.enter_context
        x1h = fent(nc.sbuf_tensor("x1h", [128, 2, 16, 512], BF16))
        hT = fent(nc.sbuf_tensor("hT", [128, 64, 512], BF16))
        w13r = fent(nc.sbuf_tensor("w13r", [128, 3, 16, 256], BF16))
        silt = fent(nc.sbuf_tensor("silt", [128, 2, 512], BF16))
        w2r = fent(nc.sbuf_tensor("w2r", [128, 4, 2, 2048], BF16))
        accT = fent(nc.sbuf_tensor("accT", [128, 16, 512], F32))
        hscr = fent(nc.sbuf_tensor("hscr", [128, 512], BF16))

        barrier = [(ST, attn_fin["ST"]), (SV, attn_fin["SV"]), (SS, attn_fin["SS"])]
        hslots = [hT[:, j, :] for j in range(NH)] + [hscr[:, :]]

        x1Rf = X1SP.rearrange("(c p) (f t) -> p f c t", p=128, f=2)
        _, x1rd_v = tick(x1S)
        em.op("sync",
              lambda e: e.dma_start(x1h[:, :, :, :], x1Rf),
              waits=barrier + [(spS, spill_vals[-1])], inc=(x1S, 16))

        w1gate = [[], [], []]
        w2gate = [[] for _ in range(4)]
        f_w2_fin = 0
        f_out_val = 0
        for f in range(NSF):
            hp_vals = [0] * NH
            vhm_hist = [0] * NH
            vsil_hist = [0] * NH
            bf_state = {"phys": list(range(NH)), "scratch": NH, "fin": 0}

            def bf_burst(mdone):
                phys = bf_state["phys"]
                first = [(SS, hp_vals[mdone])] + (dbg_gate() if KDBG == "h" else [])
                for d in (1, 2, 4, 8, 16, 32):
                    if (mdone + 1) % (2 * d) != 0:
                        continue
                    base = mdone + 1 - 2 * d
                    for j in range(base, base + d):
                        a, b = phys[j], phys[j + d]
                        em.op("vector",
                              lambda e, a=a, b=b, sc=bf_state["scratch"]: e.tensor_add(
                                  hslots[sc], hslots[a], hslots[b]),
                              waits=first)
                        first = []
                        inc, v = tick(SV)
                        em.op("vector",
                              lambda e, a=a, b=b: e.tensor_sub(hslots[b], hslots[a],
                                                               hslots[b]),
                              inc=inc)
                        phys[j] = bf_state["scratch"]
                        bf_state["scratch"] = a
                        bf_state["fin"] = v
            for m in range(NH):
                slot = m % 3
                b1 = m % 4
                b3 = 4 + (m % 4)
                _, vd3 = tick(w1S[slot])
                em.op("gpsimd",
                      lambda e, m=m, slot=slot: e.dma_start(
                          w13r[:, slot, :, :], W13R[:, :, m, :]),
                      waits=(barrier if (f == 0 and m == 0) else []) + list(w1gate[slot]),
                      inc=(w1S[slot], 16))
                s1 = s3 = 0
                for kc in range(16):
                    waits = []
                    if kc == 0:
                        waits = [(w1S[slot], vd3), (x1S, x1rd_v)] + \
                            list(bank_gate[b1]) + list(bank_gate[b3])
                        if f == 1 and m == 0:
                            waits.append((ST, f_w2_fin))
                            waits.append((outS, f_out_val))
                    inc = None
                    if kc == 15:
                        inc, s1 = tick(ST)
                    em.op("tensor",
                          lambda e, kc=kc, slot=slot, b1=b1, f=f: e.matmul(
                              ps[:, 512 * b1:512 * b1 + 512],
                              w13r[:, slot, kc, 0:128], x1h[:, f, kc, :],
                              start=(kc == 0), stop=(kc == 15)),
                          waits=waits, inc=inc)
                    inc = None
                    if kc == 15:
                        inc, s3 = tick(ST)
                    em.op("tensor",
                          lambda e, kc=kc, slot=slot, b3=b3, f=f: e.matmul(
                              ps[:, 512 * b3:512 * b3 + 512],
                              w13r[:, slot, kc, 128:256], x1h[:, f, kc, :],
                              start=(kc == 0), stop=(kc == 15)),
                          inc=inc)
                w1gate[slot] = [(ST, s3)]
                inc, vsil = tick(SS)
                em.op("scalar",
                      lambda e, m=m, b1=b1: e.activation(silt[:, m % 2, :],
                                                         ps[:, 512 * b1:512 * b1 + 512], AF.Silu),
                      waits=[(ST, s1)] + ([(SV, vhm_hist[m - 2])] if m >= 2 else []), inc=inc)
                inc, vhm = tick(SV)
                em.op("vector",
                      lambda e, m=m, b3=b3: e.tensor_mul(hT[:, m, :], silt[:, m % 2, :],
                                                         ps[:, 512 * b3:512 * b3 + 512]),
                      waits=[(ST, s3), (SS, vsil)] +
                            ([(ST, f_w2_fin)] if (f == 1 and m == 0) else []),
                      inc=inc)
                vhm_hist[m] = vhm
                bank_gate[b3] = [(SV, vhm)]
                vsil_hist[m] = vsil
                if KDBG == "hpre":
                    hp_vals[m] = 0
                    bank_gate[b1] = [(SV, vhm)]
                    continue
                def emit_tr(j):
                    bj = j % 4
                    inc, vtr = tick(ST)
                    em.op("tensor",
                          lambda e, j=j, bj=bj: e.matmul(ps[:, 512 * bj:512 * bj + 512],
                                                         h128b[:, :], hT[:, j, :],
                                                         start=True, stop=True),
                          waits=[(SV, vhm_hist[j]), (SS, vsil_hist[j]), (hadS, vHAD)], inc=inc)
                    inc, hp_vals[j] = tick(SS)
                    em.op("scalar",
                          lambda e, j=j, bj=bj: e.activation(hT[:, j, :],
                                                             ps[:, 512 * bj:512 * bj + 512],
                                                             AF.Copy, scale=ISQ),
                          waits=[(ST, vtr)], inc=inc)
                    bank_gate[bj] = [(SS, hp_vals[j])]
                if m >= 1:
                    emit_tr(m - 1)
                    if not KBFBATCH and m >= 2:
                        bf_burst(m - 2)
                if m == NH - 1:
                    for _ in range(3):
                        em.op("vector", lambda e: e.tensor_add(w_rep[:, :], zeros[:, :],
                                                               zeros[:, :]))
                    inc, vsp = tick(SV)
                    em.op("vector", lambda e: e.tensor_add(w_rep[:, :], zeros[:, :],
                                                           zeros[:, :]), inc=inc)
                    vhm_hist[m] = max(vhm_hist[m], vsp)
                    emit_tr(m)
                    if KBFBATCH:
                        for _md in range(NH):
                            bf_burst(_md)
                    else:
                        bf_burst(m - 1)
                        bf_burst(m)
            dbg_dump("h", lambda: hT[:, 0:16, :], [(SS, max(hp_vals))], cast=True, n=512)
            dbg_dump("hpre", lambda: hT[:, 0:16, :], [(SV, vhm_hist[15])], cast=True, n=512)
            phys = bf_state["phys"]
            bf_fin = bf_state["fin"]
            hp_fin_prev = bf_fin
            dbg_dump("hb2", lambda: hT[:, 0:16, :], [(SV, bf_fin)], cast=True, n=512)
            pair_retire = [0] * 32
            prev_evac = [0, 0]
            last_acc = [0] * 16
            gend = 0
            for g in range(8):
                pr_waits = []
                for p in range(4 * g, 4 * g + 4):
                    slot = p % 4
                    _, pr = tick(w2S[slot])
                    em.op("gpsimd",
                          lambda e, p=p, slot=slot: e.dma_start(
                              w2r[:, slot, :, :], W2R[:, 2 * p:2 * p + 2, :]),
                          waits=list(w2gate[slot]), inc=(w2S[slot], 16))
                    w2gate[slot] = []
                    pair_retire[p] = (w2S[slot], pr)
                g_stops = [0] * 16
                for mo in range(16):
                    bk = mo % 2
                    for jj in range(8):
                        j = 8 * g + jj
                        p = j // 2
                        waits = []
                        if jj == 0:
                            waits = [pair_retire[4 * g], pair_retire[4 * g + 1],
                                     pair_retire[4 * g + 2], pair_retire[4 * g + 3],
                                     (SV, bf_fin), (SV, prev_evac[bk])] + dbg_gate()
                            if g == 0 and mo < 2:
                                waits += bank_gate[bk]
                        inc = None
                        if jj == 7:
                            inc, g_stops[mo] = tick(ST)
                        em.op("tensor",
                              lambda e, jj=jj, mo=mo, j=j, p=p, bk=bk, ph=phys[j]: e.matmul(
                                  ps[:, 512 * bk:512 * bk + 512],
                                  w2r[:, p % 4, j % 2, 128 * mo:128 * mo + 128],
                                  hslots[ph],
                                  start=(jj == 0), stop=(jj == 7)),
                              waits=waits, inc=inc)
                    inc, v = tick(SV)
                    if g == 0:
                        em.op("vector",
                              lambda e, mo=mo, bk=bk: e.tensor_copy(
                                  accT[:, mo, :], ps[:, 512 * bk:512 * bk + 512]),
                              waits=[(ST, g_stops[mo])] +
                                    ([(outS, f_out_val)] if f == 1 and mo < 2 else []),
                              inc=inc)
                    else:
                        em.op("vector",
                              lambda e, mo=mo, bk=bk: e.tensor_add(
                                  accT[:, mo, :], accT[:, mo, :],
                                  ps[:, 512 * bk:512 * bk + 512]),
                              waits=[(ST, g_stops[mo])], inc=inc)
                    prev_evac[bk] = v
                    last_acc[mo] = v
                gend = g_stops[15]
                for p in range(4 * g, 4 * g + 4):
                    w2gate[p % 4] = [(ST, gend)]
            f_w2_fin = gend
            bank_gate[0] = [(SV, last_acc[14])]
            bank_gate[1] = [(SV, last_acc[15])]
            dbg_dump("acc", lambda: accT[:, :, :], [(SV, last_acc[15])], n=512)
            z2_vals = [0] * 16
            for mo in range(16):
                inc, z2_vals[mo] = tick(SV)
                em.op("vector",
                      lambda e, mo=mo, f=f: e.tensor_add(accT[:, mo, :], accT[:, mo, :],
                                                         x1h[:, f, mo, :]),
                      waits=[(SV, last_acc[mo])] + (dbg_gate() if mo == 0 else []), inc=inc)
            o2_vals = [0] * 16

            def ln2_writer(mu_b, rs_b, bc_ready, o2_vals=o2_vals):
                last = 0
                for mo in range(16):
                    em.op("vector",
                          lambda e, mo=mo, mu_b=mu_b: e.tensor_sub(
                              accT[:, mo, :], accT[:, mo, :], mu_b),
                          waits=[bc_ready] if mo == 0 else [])
                    inc, o2_vals[mo] = tick(SV)
                    em.op("vector",
                          lambda e, mo=mo, rs_b=rs_b: e.tensor_mul(
                              accT[:, mo, :], accT[:, mo, :], rs_b),
                          inc=inc)
                    last = o2_vals[mo]
                return last

            ln_phase(lambda mo: accT[:, mo, :], 512, (1024, 1536), (2048, 2560),
                     ln2_writer, [(SV, z2_vals[mo]) for mo in range(16)])
            _, f_out_val = tick(outS)
            em.op("sync",
                  lambda e, f=f: e.dma_start(outR[:, :, 512 * f:512 * f + 512], accT[:, :, :]),
                  waits=[(SV, o2_vals[15])], inc=(outS, 16))
        fstack.close()

        em.wait("sync", [(outS, outS.n), (spS, spS.n)])

        with nc.Block() as block:
            @block.sync
            def _(e):
                for f in em.streams["sync"]:
                    f(e)

            @block.gpsimd
            def _(e):
                for f in em.streams["gpsimd"]:
                    f(e)

            @block.tensor
            def _(e):
                for f in em.streams["tensor"]:
                    f(e)

            @block.vector
            def _(e):
                for f in em.streams["vector"]:
                    f(e)

            @block.scalar
            def _(e):
                for f in em.streams["scalar"]:
                    f(e)
    return nc


_NC_CACHE = None
LAST_RESULTS = None


def _get_nc():
    global _NC_CACHE
    if _NC_CACHE is None:
        _NC_CACHE = build_nc()
    return _NC_CACHE


def kernel(L0_attn_x, L0_attn_time_decay, L0_attn_time_first, L0_attn_tm_k, L0_attn_tm_v,
           L0_attn_tm_r, L0_attn_tm_g, L0_attn_Wr, L0_attn_Wk, L0_attn_Wv, L0_attn_Wg,
           L0_attn_Wo, L0_attn_ln_g, L0_attn_ln_b, L0_ffn_W1, L0_ffn_W2, L0_ffn_W3):
    f32 = np.float32
    x = np.asarray(L0_attn_x, f32)[0]
    xTt = np.ascontiguousarray(x.T)

    def vt(v):
        return np.ascontiguousarray(np.asarray(v, f32).reshape(16, 128).T)

    PARh = np.concatenate([vt(L0_attn_tm_k), vt(L0_attn_tm_v), vt(L0_attn_tm_r),
                           vt(L0_attn_tm_g), vt(L0_attn_time_decay), vt(L0_attn_time_first),
                           vt(L0_attn_ln_g), vt(L0_attn_ln_b)], axis=1)
    Hm = np.array([[1.0]], f32)
    while Hm.shape[0] < 128:
        Hm = np.block([[Hm, Hm], [Hm, -Hm]])
    Hm = np.ascontiguousarray(Hm.astype(f32))

    WrT = np.ascontiguousarray(np.asarray(L0_attn_Wr, f32).T)
    WkT = np.ascontiguousarray(np.asarray(L0_attn_Wk, f32).T)
    WvT = np.ascontiguousarray(np.asarray(L0_attn_Wv, f32).T)
    WgT = np.ascontiguousarray(np.asarray(L0_attn_Wg, f32).T)
    WoT = np.ascontiguousarray(np.asarray(L0_attn_Wo, f32).T)
    W1Th = np.asarray(L0_ffn_W1, f32).T.reshape(C, 64, 128)
    W3Th = np.asarray(L0_ffn_W3, f32).T.reshape(C, 64, 128)
    W13 = np.ascontiguousarray(np.concatenate([W1Th, W3Th], axis=2)).reshape(C, 64, 256)
    W2T = np.ascontiguousarray(np.asarray(L0_ffn_W2, f32).T)

    in_maps = []
    for c in range(NCORES):
        s0 = TPC * c
        if c == 0:
            sl = np.concatenate([np.zeros((C, WARM), f32), xTt[:, 1:2], xTt[:, 0:TPC]], axis=1)
            wm = np.zeros((128, 1), f32)
        else:
            sl = np.ascontiguousarray(xTt[:, s0 - WARM - 1: s0 + TPC])
            wm = np.ones((128, 1), f32)
        in_maps.append({
            "xT": sl, "WrT": WrT, "WkT": WkT, "WvT": WvT, "WgT": WgT, "WoT": WoT,
            "W13": W13, "W2T": W2T,
            "PAR": np.ascontiguousarray(np.concatenate([PARh, wm], axis=1)),
            "HAD": Hm,
        })

    nc = _get_nc()
    res = run_bass_kernel_spmd(nc, in_maps, core_ids=list(range(NCORES)))
    global LAST_RESULTS
    LAST_RESULTS = res.results
    outs = [np.asarray(r["OUT"]) for r in res.results]
    full = np.concatenate(outs, axis=1)
    return np.ascontiguousarray(full.T)[None].astype(np.float32)


# version 05
# speedup vs baseline: 1.2851x; 1.1691x over previous
import numpy as np
import concourse.bass as bass
import concourse.mybir as mybir
from concourse.bass_utils import run_bass_kernel_spmd
from contextlib import ExitStack

F32 = mybir.dt.float32
BF16 = mybir.dt.bfloat16
AF = mybir.ActivationFunctionType
OP = mybir.AluOpType

C = 2048
T = 8192
NCORES = 8
TPC = T // NCORES
KC = 16
TA = 256
NSA = TPC // TA
WARM = 128
TF = 512
NSF = TPC // TF
H = 8192
NH = H // 128
EPS = 1e-5
ISQ = float(1.0 / np.sqrt(8192.0))
XCOLS = 1 + WARM + TPC


class Sem:
    def __init__(self, h, name, step=1):
        self.h, self.name, self.n, self.step = h, name, 0, step

    def tick(self):
        self.n += self.step
        return self.n


class Em:
    ENGS = ("sync", "gpsimd", "tensor", "vector", "scalar")

    def __init__(self):
        self.streams = {k: [] for k in self.ENGS}
        self.wm = {k: {} for k in self.ENGS}

    def op(self, eng, fn, waits=(), inc=None):
        wl = []
        wm = self.wm[eng]
        for sw in waits:
            if sw is None:
                continue
            sem, val = sw
            if val is None or val <= 0:
                continue
            if val > wm.get(sem.name, 0):
                wm[sem.name] = val
                wl.append((sem, val))

        def run(e, fn=fn, wl=wl, inc=inc):
            for sem, val in wl:
                e.wait_ge(sem.h, val)
            ins = fn(e)
            if inc is not None:
                sem, k = inc
                ins.then_inc(sem.h, k)
        self.streams[eng].append(run)

    def wait(self, eng, waits):
        wl = []
        wm = self.wm[eng]
        for sem, val in waits:
            if val and val > wm.get(sem.name, 0):
                wm[sem.name] = val
                wl.append((sem, val))
        if wl:
            def run(e, wl=wl):
                for sem, val in wl:
                    e.wait_ge(sem.h, val)
            self.streams[eng].append(run)


import os
KDBG = os.environ.get("KDBG", "")
KBFBATCH = bool(os.environ.get("KBFBATCH", ""))


def build_nc():
    nc = bass.Bass("TRN2", target_bir_lowering=False)
    xT = nc.dram_tensor("xT", [C, XCOLS], F32, kind="ExternalInput").ap()
    WrT = nc.dram_tensor("WrT", [C, C], F32, kind="ExternalInput").ap()
    WkT = nc.dram_tensor("WkT", [C, C], F32, kind="ExternalInput").ap()
    WvT = nc.dram_tensor("WvT", [C, C], F32, kind="ExternalInput").ap()
    WgT = nc.dram_tensor("WgT", [C, C], F32, kind="ExternalInput").ap()
    WoT = nc.dram_tensor("WoT", [C, C], F32, kind="ExternalInput").ap()
    W13 = nc.dram_tensor("W13", [C, 64, 256], F32, kind="ExternalInput").ap()
    W2T = nc.dram_tensor("W2T", [H, C], F32, kind="ExternalInput").ap()
    PAR = nc.dram_tensor("PAR", [128, 129], F32, kind="ExternalInput").ap()
    HAD = nc.dram_tensor("HAD", [128, 128], F32, kind="ExternalInput").ap()
    OUT = nc.dram_tensor("OUT", [C, TPC], F32, kind="ExternalOutput").ap()
    X1SP = nc.dram_tensor("X1SP", [C, TPC], BF16, kind="Internal").ap()

    xTr = xT.rearrange("(c p) t -> p c t", p=128)
    outR = OUT.rearrange("(c p) t -> p c t", p=128)
    x1R = X1SP.rearrange("(c p) t -> p c t", p=128)
    W13R = W13.rearrange("(c p) m n -> p c m n", p=128)
    W2R = W2T.rearrange("(c p) n -> p c n", p=128)

    em = Em()
    dbg_state = {"ap": None, "gate": None, "used": False}
    if KDBG:
        dbg_state["ap"] = nc.dram_tensor("DBG", [128, 16, 512], F32, kind="ExternalOutput").ap()
    with ExitStack() as stack:
        ent = stack.enter_context
        def dsem(name):
            return Sem(ent(nc.semaphore(name)), name, 16)
        wringS = [dsem("A%d" % i) for i in range(4)]
        w1S = [dsem("B%d" % i) for i in range(3)]
        w2S = [dsem("C%d" % i) for i in range(8)]
        xtS = dsem("XT")
        xfS = dsem("XF")
        x1S = dsem("X1")
        spS = dsem("SP")
        outS = dsem("OU")
        parS = dsem("PA")
        hadS = dsem("HA")
        dbgS = dsem("DB")
        ST = Sem(ent(nc.semaphore("ST")), "ST")
        SV = Sem(ent(nc.semaphore("SV")), "SV")
        SS = Sem(ent(nc.semaphore("SS")), "SS")

        def tick(sem):
            v = sem.tick()
            return (sem, sem.step), v

        def dbg_dump(tag, src_fn, waits, cast=False, n=256):
            if KDBG != tag or dbg_state["used"]:
                return
            dbg_state["used"] = True
            _, v = tick(dbgS)
            em.op("gpsimd" if cast else "sync",
                  lambda e: e.dma_start(dbg_state["ap"][:, :, 0:n], src_fn()),
                  waits=waits, inc=(dbgS, 16))
            dbg_state["gate"] = (dbgS, v)

        def dbg_gate():
            g = dbg_state["gate"]
            return [g] if g is not None else []

        ps = ent(nc.psum_tensor("ps", [128, 4096], F32))
        h128b = ent(nc.sbuf_tensor("h128b", [128, 128], BF16))
        par = ent(nc.sbuf_tensor("par", [128, 129], F32))
        cns = ent(nc.sbuf_tensor("cns", [128, 48], F32))
        zeros = ent(nc.sbuf_tensor("zeros", [128, 256], F32))
        w_rep = ent(nc.sbuf_tensor("w_rep", [128, 256], F32))
        carA = ent(nc.sbuf_tensor("carA", [128, 16], F32))
        carB = ent(nc.sbuf_tensor("carB", [128, 16], F32))
        statf = ent(nc.sbuf_tensor("statf", [1, 4, 512], F32))
        statb = ent(nc.sbuf_tensor("statb", [1, 2, 512], BF16))
        lnrA = ent(nc.sbuf_tensor("lnrA", [128, 2, 512], BF16))
        lnrB = ent(nc.sbuf_tensor("lnrB", [128, 2, 512], BF16))

        tm = {k: par[:, 16 * i:16 * (i + 1)] for i, k in
              enumerate(["k", "v", "r", "g", "td", "tf", "lng", "lnb"])}
        w_c = cns[:, 0:16]
        eu_c = cns[:, 16:32]
        c2_c = cns[:, 32:48]

        wmsk = par[:, 128:129]
        _, vPAR = tick(parS)
        em.op("sync", lambda e: e.dma_start(par[:, :], PAR), inc=(parS, 16))
        vWM = vPAR
        _, vHAD = tick(hadS)
        em.op("gpsimd", lambda e: e.dma_start(h128b[:, :], HAD), inc=(hadS, 16))
        ones_col = h128b[:, 0:1]
        ones_row = h128b[0:1, :]

        em.op("scalar", lambda e: e.activation(cns[:, 0:16], tm["td"], AF.Exp),
              waits=[(parS, vPAR)])
        em.op("scalar", lambda e: e.activation(cns[:, 0:16], cns[:, 0:16], AF.Exp, scale=-1.0))
        inc, vEU = tick(SS)
        em.op("scalar", lambda e: e.activation(cns[:, 16:32], tm["tf"], AF.Exp), inc=inc)
        em.op("vector", lambda e: e.tensor_mul(cns[:, 32:48], cns[:, 0:16], cns[:, 16:32]),
              waits=[(SS, vEU)])
        inc, vZ = tick(SV)
        em.op("vector", lambda e: e.memset(zeros[:, :], 0.0), inc=inc)
        em.op("vector", lambda e: e.memset(w_rep[:, :], 0.0))
        inc, vC2 = tick(SV)
        em.op("vector", lambda e: e.tensor_scalar(cns[:, 32:48], cns[:, 32:48], -1.0, 1.0,
                                                  OP.mult, OP.add), inc=inc)

        bank_gate = [[] for _ in range(8)]
        last_ln_roll = [0]

        def ln_phase(src_fn, N, st_cols, bc_cols, out_writer, src_ready):
            cp_vals = [0] * 16
            rfv = [0] * 16
            for mo in range(16):
                w = [src_ready[mo]] if src_ready else []
                if mo >= 2:
                    w.append((ST, rfv[mo - 2]))
                else:
                    w.append((ST, last_ln_roll[0]))
                em.op("scalar",
                      lambda e, mo=mo: e.activation(lnrA[:, mo % 2, 0:N], src_fn(mo), AF.Copy),
                      waits=w)
                inc, cp_vals[mo] = tick(SS)
                em.op("scalar",
                      lambda e, mo=mo: e.activation(lnrB[:, mo % 2, 0:N], src_fn(mo), AF.Square),
                      inc=inc)
            for mo in range(16):
                waits = [(SS, cp_vals[mo])]
                if mo == 0:
                    waits += bank_gate[st_cols[0] // 512] + bank_gate[st_cols[1] // 512]
                sb_same = (st_cols[0] // 512) == (st_cols[1] // 512)
                em.op("tensor",
                      lambda e, mo=mo: e.matmul(ps[0:1, st_cols[0]:st_cols[0] + N], ones_col,
                                                lnrA[:, mo % 2, 0:N],
                                                start=(mo == 0), stop=(mo == 15)),
                      waits=waits)
                inc, rfv[mo] = tick(ST)
                em.op("tensor",
                      lambda e, mo=mo, sb_same=sb_same: e.matmul(
                          ps[0:1, st_cols[1]:st_cols[1] + N], ones_col,
                          lnrB[:, mo % 2, 0:N],
                          start=(mo == 0 and not sb_same), stop=(mo == 15)),
                      inc=inc)
            last_ln_roll[0] = rfv[15]
            st_stop = rfv[15]
            em.op("vector",
                  lambda e: e.tensor_scalar(statf[0:1, 0, 0:N], ps[0:1, st_cols[0]:st_cols[0] + N],
                                            1.0 / 2048.0, None, OP.mult),
                  waits=[(ST, st_stop)])
            em.op("vector",
                  lambda e: e.tensor_scalar(statf[0:1, 1, 0:N], ps[0:1, st_cols[1]:st_cols[1] + N],
                                            1.0 / 2048.0, EPS, OP.mult, OP.add))
            em.op("vector", lambda e: e.tensor_mul(statf[0:1, 2, 0:N], statf[0:1, 0, 0:N],
                                                   statf[0:1, 0, 0:N]))
            inc, vmunge = tick(SV)
            em.op("vector", lambda e: e.tensor_sub(statf[0:1, 1, 0:N], statf[0:1, 1, 0:N],
                                                   statf[0:1, 2, 0:N]), inc=inc)
            bank_gate[st_cols[0] // 512] = [(SV, vmunge)]
            bank_gate[st_cols[1] // 512] = [(SV, vmunge)]
            em.op("scalar",
                  lambda e: e.activation(statf[0:1, 3, 0:N], statf[0:1, 1, 0:N], AF.Ln),
                  waits=[(SV, vmunge)])
            em.op("scalar", lambda e: e.activation(statf[0:1, 3, 0:N], statf[0:1, 3, 0:N],
                                                   AF.Exp, scale=-0.5))
            em.op("scalar", lambda e: e.activation(statb[0:1, 0, 0:N], statf[0:1, 0, 0:N], AF.Copy))
            inc, vcast = tick(SS)
            em.op("scalar", lambda e: e.activation(statb[0:1, 1, 0:N], statf[0:1, 3, 0:N], AF.Copy),
                  inc=inc)
            em.op("tensor",
                  lambda e: e.matmul(ps[:, bc_cols[0]:bc_cols[0] + N], ones_row, statb[0:1, 0, 0:N],
                                     start=True, stop=True),
                  waits=[(SS, vcast), (hadS, vHAD)] + bank_gate[bc_cols[0] // 512])
            inc, vbc = tick(ST)
            bc_same = (bc_cols[0] // 512) == (bc_cols[1] // 512)
            em.op("tensor",
                  lambda e, bc_same=bc_same: e.matmul(
                      ps[:, bc_cols[1]:bc_cols[1] + N], ones_row, statb[0:1, 1, 0:N],
                      start=(not bc_same), stop=True),
                  waits=bank_gate[bc_cols[1] // 512], inc=inc)
            mu_b = ps[:, bc_cols[0]:bc_cols[0] + N]
            rs_b = ps[:, bc_cols[1]:bc_cols[1] + N]
            last = out_writer(mu_b, rs_b, (ST, vbc))
            bank_gate[bc_cols[0] // 512] = [(SV, last)]
            bank_gate[bc_cols[1] // 512] = [(SV, last)]

        astack = ExitStack()
        aent = astack.enter_context
        xtile = aent(nc.sbuf_tensor("xtile", [128, 16, 385], BF16))
        xf32 = aent(nc.sbuf_tensor("xf32", [128, 16, 256], F32))
        xx_r = aent(nc.sbuf_tensor("xx_r", [128, 256], F32))
        stg_k = aent(nc.sbuf_tensor("stg_k", [128, 16, 256], BF16))
        stg_v = aent(nc.sbuf_tensor("stg_v", [128, 16, 256], BF16))
        stg_r = aent(nc.sbuf_tensor("stg_r", [128, 16, 256], BF16))
        stg_g = aent(nc.sbuf_tensor("stg_g", [128, 16, 256], BF16))
        ekT = aent(nc.sbuf_tensor("ekT", [128, 16, 256], F32))
        uT = aent(nc.sbuf_tensor("uT", [128, 16, 256], F32))
        apg = aent(nc.sbuf_tensor("apg", [128, 2, 257], F32))
        bpg = aent(nc.sbuf_tensor("bpg", [128, 2, 257], F32))
        numb = aent(nc.sbuf_tensor("numb", [128, 2, 256], F32))
        denb = aent(nc.sbuf_tensor("denb", [128, 2, 256], F32))
        invb = aent(nc.sbuf_tensor("invb", [128, 2, 256], F32))
        rsig = aent(nc.sbuf_tensor("rsig", [128, 16, 256], BF16))
        gsr = aent(nc.sbuf_tensor("gsr", [128, 2, 256], BF16))
        wring = aent(nc.sbuf_tensor("wring", [128, 4, 2, 2048], BF16))
        attn_b16 = stg_r
        x1sub = stg_g

        wring_gate = [[] for _ in range(4)]
        spill_vals = []

        def proj(wdram, rhs_fn, N, rhs_ready):
            wR = wdram.rearrange("(c p) n -> p c n", p=128)
            dv = [0] * 8

            def emit_dma(p, waits):
                slot = p % 4
                _, dv[p] = tick(wringS[slot])
                em.op("gpsimd",
                      lambda e, p=p, slot=slot, wR=wR: e.dma_start(
                          wring[:, slot, :, :], wR[:, 2 * p:2 * p + 2, :]),
                      waits=waits, inc=(wringS[slot], 16))

            for p in range(4):
                emit_dma(p, list(wring_gate[p]))
                wring_gate[p] = []
            stop_vals = [0] * 16
            kc_tick = [0] * 16
            for kc in range(16):
                p, i = kc // 2, kc % 2
                slot = p % 4
                for mo in range(16):
                    waits = []
                    if mo == 0:
                        waits.append((wringS[slot], dv[p]))
                        if kc == 0:
                            waits += list(rhs_ready)
                    if kc == 0:
                        waits += bank_gate[mo // 2]
                    inc = None
                    if kc == 15:
                        inc, stop_vals[mo] = tick(ST)
                    elif kc % 2 == 1 and mo == 15:
                        inc, kc_tick[kc] = tick(ST)
                    em.op("tensor",
                          lambda e, kc=kc, i=i, slot=slot, mo=mo, N=N, rhs_fn=rhs_fn: e.matmul(
                              ps[:, 256 * mo:256 * mo + N],
                              wring[:, slot, i, 128 * mo:128 * mo + 128],
                              rhs_fn(kc),
                              start=(kc == 0 and mo % 2 == 0), stop=(kc == 15)),
                          waits=waits, inc=inc)
                if kc in (1, 3, 5, 7):
                    emit_dma(4 + (kc - 1) // 2, [(ST, kc_tick[kc])])
            for b in range(8):
                bank_gate[b] = []
            wring_gate[0] = [(ST, kc_tick[9])]
            wring_gate[1] = [(ST, kc_tick[11])]
            wring_gate[2] = [(ST, kc_tick[13])]
            wring_gate[3] = [(ST, stop_vals[15])]
            return stop_vals

        def lerp_phase(base, n, sigs, xt_ready, extra=()):
            fin = 0
            for kc in range(16):
                w = list(extra) + [xt_ready] if kc == 0 else []
                em.op("vector",
                      lambda e, kc=kc, base=base, n=n: e.tensor_sub(
                          xx_r[:, 0:n], xtile[:, kc, base:base + n],
                          xtile[:, kc, base + 1:base + 1 + n]),
                      waits=w)
                for name, dst in sigs:
                    inc, fin = tick(SV)
                    em.op("vector",
                          lambda e, kc=kc, base=base, n=n, name=name, dst=dst:
                          e.scalar_tensor_tensor(
                              dst[:, kc, 0:n], xx_r[:, 0:n], tm[name][:, kc:kc + 1],
                              xtile[:, kc, base + 1:base + 1 + n], OP.mult, OP.add),
                          inc=inc)
            return fin

        _, vXT = tick(xtS)
        em.op("gpsimd", lambda e: e.dma_start(xtile[:, :, 0:385], xTr[:, :, 0:385]),
              inc=(xtS, 16))
        xt_ready = (xtS, vXT)

        lerp_fin = lerp_phase(0, 128, [("k", stg_k), ("v", stg_v)], xt_ready)
        kstops = proj(WkT, lambda kc: stg_k[:, kc, 0:128], 128, [(SV, lerp_fin)])
        ek_vals = [0] * 16
        for mo in range(16):
            inc, ek_vals[mo] = tick(SS)
            em.op("scalar",
                  lambda e, mo=mo: e.activation(ekT[:, mo, 0:128],
                                                ps[:, 256 * mo:256 * mo + 128], AF.Exp),
                  waits=[(ST, kstops[mo | 1])], inc=inc)
            bank_gate[mo // 2] = [(SS, ek_vals[mo])]
        vstops = proj(WvT, lambda kc: stg_v[:, kc, 0:128], 128, [(SV, lerp_fin)])
        u_vals = [0] * 16
        for mo in range(16):
            inc, u_vals[mo] = tick(SV)
            em.op("vector",
                  lambda e, mo=mo: e.tensor_mul(uT[:, mo, 0:128],
                                                ps[:, 256 * mo:256 * mo + 128],
                                                ekT[:, mo, 0:128]),
                  waits=[(ST, vstops[mo | 1]), (SS, ek_vals[mo])], inc=inc)
            bank_gate[mo // 2] = [(SV, u_vals[mo])]
        def pro_stage(mo):
            g = mo % 2
            em.op("vector",
                  lambda e, mo=mo: e.scalar_tensor_tensor(
                      w_rep[:, 0:128], zeros[:, 0:128], w_c[:, mo:mo + 1],
                      zeros[:, 0:128], OP.add, OP.add),
                  waits=[(SV, u_vals[mo]), (SV, vZ), (SV, vC2)])
            em.op("vector",
                  lambda e, mo=mo, g=g: e.tensor_tensor_scan(
                      apg[:, g, 0:128], w_rep[:, 0:128], uT[:, mo, 0:128], 0.0,
                      OP.mult, OP.add))
            em.op("vector",
                  lambda e, mo=mo, g=g: e.tensor_tensor_scan(
                      bpg[:, g, 0:128], w_rep[:, 0:128], ekT[:, mo, 0:128], 0.0,
                      OP.mult, OP.add))

        def pro_consume(mo):
            g = mo % 2
            em.op("vector", lambda e, mo=mo, g=g: e.tensor_copy(carA[:, mo:mo + 1],
                                                                apg[:, g, 127:128]))
            em.op("vector", lambda e, mo=mo, g=g: e.tensor_copy(carB[:, mo:mo + 1],
                                                                bpg[:, g, 127:128]))

        pro_stage(0)
        for mo in range(1, 16):
            pro_stage(mo)
            pro_consume(mo - 1)
        pro_consume(15)
        em.op("vector", lambda e: e.tensor_scalar(carA[:, :], carA[:, :], wmsk[:, 0:1],
                                                  None, OP.mult),
              waits=[(parS, vWM)])
        inc, vCAR = tick(SV)
        em.op("vector", lambda e: e.tensor_scalar(carB[:, :], carB[:, :], wmsk[:, 0:1],
                                                  None, OP.mult), inc=inc)

        prev_z_fin = 0
        prev_attnmul_fin = vCAR
        prev_uT_guard = vCAR
        prev_lerp_fin = lerp_fin
        pro_stop = vstops[15]
        for s in range(NSA):
            base = 128 if s == 0 else 0
            if s > 0:
                _, vXT = tick(xtS)
                em.op("gpsimd",
                      lambda e, s=s: e.dma_start(xtile[:, :, 0:257],
                                                 xTr[:, :, 128 + 256 * s:385 + 256 * s]),
                      waits=[(SV, prev_lerp_fin)], inc=(xtS, 16))
                xt_ready = (xtS, vXT)
            _, vXF = tick(xfS)
            em.op("sync",
                  lambda e, s=s: e.dma_start(xf32[:, :, :],
                                             xTr[:, :, 129 + 256 * s:385 + 256 * s]),
                  waits=[(SV, prev_z_fin)], inc=(xfS, 16))
            extra = [(ST, pro_stop)] if s == 0 else [(spS, spill_vals[-1])]
            lerp_fin = lerp_phase(base, 256,
                                  [("k", stg_k), ("v", stg_v), ("r", stg_r), ("g", stg_g)],
                                  xt_ready, extra=extra)
            prev_lerp_fin = lerp_fin
            kstops = proj(WkT, lambda kc: stg_k[:, kc, 0:256], 256, [(SV, lerp_fin)])
            ek_vals = [0] * 16
            for mo in range(16):
                inc, ek_vals[mo] = tick(SS)
                em.op("scalar",
                      lambda e, mo=mo: e.activation(ekT[:, mo, 0:256],
                                                    ps[:, 256 * mo:256 * mo + 256], AF.Exp),
                      waits=[(ST, kstops[mo | 1]), (SV, prev_attnmul_fin)], inc=inc)
                bank_gate[mo // 2] = [(SS, ek_vals[mo])]
            vstops = proj(WvT, lambda kc: stg_v[:, kc, 0:256], 256, [(SV, lerp_fin)])
            u_vals = [0] * 16
            for mo in range(16):
                inc, u_vals[mo] = tick(SV)
                em.op("vector",
                      lambda e, mo=mo: e.tensor_mul(uT[:, mo, 0:256],
                                                    ps[:, 256 * mo:256 * mo + 256],
                                                    ekT[:, mo, 0:256]),
                      waits=[(ST, vstops[mo | 1]), (SS, ek_vals[mo]), (SV, prev_uT_guard)],
                      inc=inc)
                bank_gate[mo // 2] = [(SV, u_vals[mo])]
            dbg_dump("stgk", lambda: stg_k[:, :, 0:256].rearrange("p c t -> p c t"),
                     [(SV, lerp_fin)], cast=True)
            dbg_dump("ek", lambda: ekT[:, :, :], [(SS, ek_vals[15])])
            dbg_dump("u", lambda: uT[:, :, :], [(SV, u_vals[15])])
            wkv_vals = [0] * 16
            vinv_hist = [0] * 16

            def sc_stage(mo, s=s):
                g = mo % 2
                em.op("vector",
                      lambda e, mo=mo: e.scalar_tensor_tensor(
                          w_rep[:, 0:256], zeros[:, 0:256], w_c[:, mo:mo + 1],
                          zeros[:, 0:256], OP.add, OP.add),
                      waits=[(SV, u_vals[mo])] + ([(SV, vCAR)] if (s == 0 and mo == 0) else []))
                em.op("vector", lambda e, mo=mo, g=g: e.tensor_copy(apg[:, g, 0:1],
                                                                    carA[:, mo:mo + 1]))
                em.op("vector",
                      lambda e, mo=mo, g=g: e.tensor_tensor_scan(
                          apg[:, g, 1:257], w_rep[:, 0:256], uT[:, mo, 0:256],
                          carA[:, mo:mo + 1], OP.mult, OP.add))
                em.op("vector", lambda e, mo=mo, g=g: e.tensor_copy(bpg[:, g, 0:1],
                                                                    carB[:, mo:mo + 1]))
                em.op("vector",
                      lambda e, mo=mo, g=g: e.tensor_tensor_scan(
                          bpg[:, g, 1:257], w_rep[:, 0:256], ekT[:, mo, 0:256],
                          carB[:, mo:mo + 1], OP.mult, OP.add))

            def sc_consume(mo):
                g = mo % 2
                em.op("vector", lambda e, mo=mo, g=g: e.tensor_copy(carA[:, mo:mo + 1],
                                                                    apg[:, g, 256:257]))
                em.op("vector", lambda e, mo=mo, g=g: e.tensor_copy(carB[:, mo:mo + 1],
                                                                    bpg[:, g, 256:257]))
                em.op("vector",
                      lambda e, mo=mo, g=g: e.tensor_scalar(
                          numb[:, g, :], apg[:, g, 1:257], eu_c[:, mo:mo + 1], None, OP.mult))
                em.op("vector",
                      lambda e, mo=mo, g=g: e.scalar_tensor_tensor(
                          numb[:, g, :], apg[:, g, 0:256], c2_c[:, mo:mo + 1],
                          numb[:, g, :], OP.mult, OP.add))
                em.op("vector",
                      lambda e, mo=mo, g=g: e.tensor_scalar(
                          denb[:, g, :], bpg[:, g, 1:257], eu_c[:, mo:mo + 1], None, OP.mult),
                      waits=([(SS, vinv_hist[mo - 2])] if mo >= 2 else []))
                inc, vden = tick(SV)
                em.op("vector",
                      lambda e, mo=mo, g=g: e.scalar_tensor_tensor(
                          denb[:, g, :], bpg[:, g, 0:256], c2_c[:, mo:mo + 1],
                          denb[:, g, :], OP.mult, OP.add), inc=inc)
                em.op("scalar",
                      lambda e, mo=mo, g=g: e.activation(invb[:, g, :], denb[:, g, :], AF.Ln),
                      waits=[(SV, vden)])
                inc, vinv = tick(SS)
                vinv_hist[mo] = vinv
                em.op("scalar",
                      lambda e, mo=mo, g=g: e.activation(invb[:, g, :], invb[:, g, :],
                                                         AF.Exp, scale=-1.0), inc=inc)
                inc, wkv_vals[mo] = tick(SV)
                em.op("vector",
                      lambda e, mo=mo, g=g: e.tensor_mul(ekT[:, mo, 0:256], numb[:, g, :],
                                                         invb[:, g, :]),
                      waits=[(SS, vinv)] + (dbg_gate() if mo == 0 else []), inc=inc)

            sc_stage(0)
            for mo in range(1, 16):
                sc_stage(mo)
                sc_consume(mo - 1)
            sc_consume(15)
            dbg_dump("wkv", lambda: ekT[:, :, :], [(SV, wkv_vals[15])])
            rstops = proj(WrT, lambda kc: stg_r[:, kc, 0:256], 256, [(SV, lerp_fin)])
            rs_vals = [0] * 16
            for mo in range(16):
                inc, rs_vals[mo] = tick(SS)
                em.op("scalar",
                      lambda e, mo=mo: e.activation(rsig[:, mo, 0:256],
                                                    ps[:, 256 * mo:256 * mo + 256], AF.Sigmoid),
                      waits=[(ST, rstops[mo | 1])], inc=inc)
                bank_gate[mo // 2] = [(SS, rs_vals[mo])]
            am_vals = [0] * 16
            for mo in range(16):
                inc, am_vals[mo] = tick(SV)
                em.op("vector",
                      lambda e, mo=mo: e.tensor_mul(ekT[:, mo, 0:256], ekT[:, mo, 0:256],
                                                    rsig[:, mo, 0:256]),
                      waits=[(SS, rs_vals[mo]), (SV, wkv_vals[mo])] +
                            (dbg_gate() if mo == 0 else []), inc=inc)
            gstops = proj(WgT, lambda kc: stg_g[:, kc, 0:256], 256, [(SV, lerp_fin)])
            ab_vals = [0] * 16
            for mo in range(16):
                inc, vgs = tick(SS)
                em.op("scalar",
                      lambda e, mo=mo: e.activation(gsr[:, mo % 2, :],
                                                    ps[:, 256 * mo:256 * mo + 256], AF.Sigmoid),
                      waits=[(ST, gstops[mo | 1])] +
                            ([(SV, ab_vals[mo - 2])] if mo >= 2 else []), inc=inc)
                bank_gate[mo // 2] = [(SS, vgs)]
                inc, ab_vals[mo] = tick(SV)
                em.op("vector",
                      lambda e, mo=mo: e.tensor_mul(attn_b16[:, mo, 0:256], ekT[:, mo, 0:256],
                                                    gsr[:, mo % 2, :]),
                      waits=[(SS, vgs), (SV, am_vals[mo])], inc=inc)
            prev_attnmul_fin = ab_vals[15]
            dbg_dump("attnin", lambda: attn_b16[:, :, 0:256], [(SV, ab_vals[15])], cast=True)
            ostops = proj(WoT, lambda kc: attn_b16[:, kc, 0:256], 256, [(SV, ab_vals[15])])
            o_vals = [0] * 16
            for mo in range(16):
                inc, o_vals[mo] = tick(SS)
                em.op("scalar",
                      lambda e, mo=mo: e.activation(uT[:, mo, 0:256],
                                                    ps[:, 256 * mo:256 * mo + 256], AF.Copy),
                      waits=[(ST, ostops[mo | 1])], inc=inc)
                bank_gate[mo // 2] = [(SS, o_vals[mo])]
            dbg_dump("o", lambda: uT[:, :, :], [(SS, o_vals[15])])
            z_vals = [0] * 16

            def attn_writer(mu_b, rs_b, bc_ready, vXF=vXF, z_vals=z_vals):
                last = 0
                for mo in range(16):
                    em.op("vector",
                          lambda e, mo=mo, mu_b=mu_b: e.tensor_sub(
                              numb[:, mo % 2, :], uT[:, mo, 0:256], mu_b),
                          waits=[bc_ready] if mo == 0 else [])
                    em.op("vector",
                          lambda e, mo=mo, rs_b=rs_b: e.tensor_mul(
                              numb[:, mo % 2, :], numb[:, mo % 2, :], rs_b))
                    em.op("vector",
                          lambda e, mo=mo: e.tensor_scalar(
                              numb[:, mo % 2, :], numb[:, mo % 2, :],
                              tm["lng"][:, mo:mo + 1], tm["lnb"][:, mo:mo + 1],
                              OP.mult, OP.add))
                    inc, z_vals[mo] = tick(SV)
                    em.op("vector",
                          lambda e, mo=mo: e.tensor_add(uT[:, mo, 0:256], numb[:, mo % 2, :],
                                                        xf32[:, mo, :]),
                          waits=[(xfS, vXF)] + (dbg_gate() if mo == 0 else []), inc=inc)
                    last = z_vals[mo]
                return last

            ln_phase(lambda mo: uT[:, mo, 0:256], 256, (0, 256), (512, 768),
                     attn_writer, [(SS, o_vals[mo]) for mo in range(16)])
            prev_z_fin = z_vals[15]
            x1_vals = [0] * 16

            def ln1_writer(mu_b, rs_b, bc_ready, x1_vals=x1_vals, gstop=gstops[15]):
                last = 0
                for mo in range(16):
                    em.op("vector",
                          lambda e, mo=mo, mu_b=mu_b: e.tensor_sub(
                              numb[:, mo % 2, :], uT[:, mo, 0:256], mu_b),
                          waits=[bc_ready] if mo == 0 else [])
                    inc, x1_vals[mo] = tick(SV)
                    em.op("vector",
                          lambda e, mo=mo, rs_b=rs_b: e.tensor_mul(
                              x1sub[:, mo, 0:256], numb[:, mo % 2, :], rs_b),
                          waits=[(ST, gstop)], inc=inc)
                    last = x1_vals[mo]
                return last

            dbg_dump("z", lambda: uT[:, :, :], [(SV, z_vals[15])])
            ln_phase(lambda mo: uT[:, mo, 0:256], 256, (1024, 1280), (1536, 1792),
                     ln1_writer, [(SV, z_vals[mo]) for mo in range(16)])
            prev_uT_guard = x1_vals[15]
            dbg_dump("x1", lambda: x1sub[:, :, 0:256], [(SV, x1_vals[15])], cast=True)
            prev_sp = [(spS, spill_vals[-1])] if spill_vals else []
            _, vSP = tick(spS)
            em.op("sync",
                  lambda e, s=s: e.dma_start(x1R[:, :, 256 * s:256 * s + 256],
                                             x1sub[:, :, 0:256]),
                  waits=[(SV, x1_vals[15])] + dbg_gate() + prev_sp, inc=(spS, 16))
            spill_vals.append(vSP)

        attn_fin = {"ST": ST.n, "SV": SV.n, "SS": SS.n}
        astack.close()

        fstack = ExitStack()
        fent = fstack.enter_context
        x1h = fent(nc.sbuf_tensor("x1h", [128, 2, 16, 512], BF16))
        hT = fent(nc.sbuf_tensor("hT", [128, 64, 512], BF16))
        w13r = fent(nc.sbuf_tensor("w13r", [128, 3, 16, 256], BF16))
        silt = fent(nc.sbuf_tensor("silt", [128, 2, 512], BF16))
        w2r = fent(nc.sbuf_tensor("w2r", [128, 8, 2048], BF16))
        accT = fent(nc.sbuf_tensor("accT", [128, 16, 512], F32))
        hscr = fent(nc.sbuf_tensor("hscr", [128, 512], BF16))

        barrier = [(ST, attn_fin["ST"]), (SV, attn_fin["SV"]), (SS, attn_fin["SS"])]
        hslots = [hT[:, j, :] for j in range(NH)] + [hscr[:, :]]

        x1Rf = X1SP.rearrange("(c p) (f t) -> p f c t", p=128, f=2)
        _, x1rd_v = tick(x1S)
        em.op("sync",
              lambda e: e.dma_start(x1h[:, :, :, :], x1Rf),
              waits=barrier + [(spS, spill_vals[-1])], inc=(x1S, 16))

        w1gate = [[], [], []]
        w2gate = [[] for _ in range(8)]
        f_w2_fin = 0
        f_out_val = 0
        for f in range(NSF):
            hp_vals = [0] * NH
            vhm_hist = [0] * NH
            vsil_hist = [0] * NH
            bf_state = {"phys": list(range(NH)), "scratch": NH, "fin": 0}

            def bf_burst(mdone):
                phys = bf_state["phys"]
                first = [(SS, hp_vals[mdone])] + (dbg_gate() if KDBG == "h" else [])
                for d in (1, 2, 4, 8, 16, 32):
                    if (mdone + 1) % (2 * d) != 0:
                        continue
                    base = mdone + 1 - 2 * d
                    for j in range(base, base + d):
                        a, b = phys[j], phys[j + d]
                        em.op("vector",
                              lambda e, a=a, b=b, sc=bf_state["scratch"]: e.tensor_add(
                                  hslots[sc], hslots[a], hslots[b]),
                              waits=first)
                        first = []
                        inc, v = tick(SV)
                        em.op("vector",
                              lambda e, a=a, b=b: e.tensor_sub(hslots[b], hslots[a],
                                                               hslots[b]),
                              inc=inc)
                        phys[j] = bf_state["scratch"]
                        bf_state["scratch"] = a
                        bf_state["fin"] = v
            for m in range(NH):
                slot = m % 3
                b1 = m % 4
                b3 = 4 + (m % 4)
                _, vd3 = tick(w1S[slot])
                em.op("gpsimd",
                      lambda e, m=m, slot=slot: e.dma_start(
                          w13r[:, slot, :, :], W13R[:, :, m, :]),
                      waits=(barrier if (f == 0 and m == 0) else []) + list(w1gate[slot]),
                      inc=(w1S[slot], 16))
                s1 = s3 = 0
                for kc in range(16):
                    waits = []
                    if kc == 0:
                        waits = [(w1S[slot], vd3), (x1S, x1rd_v)] + \
                            list(bank_gate[b1]) + list(bank_gate[b3])
                        if f == 1 and m == 0:
                            waits.append((ST, f_w2_fin))
                            waits.append((outS, f_out_val))
                    inc = None
                    if kc == 15:
                        inc, s1 = tick(ST)
                    em.op("tensor",
                          lambda e, kc=kc, slot=slot, b1=b1, f=f: e.matmul(
                              ps[:, 512 * b1:512 * b1 + 512],
                              w13r[:, slot, kc, 0:128], x1h[:, f, kc, :],
                              start=(kc == 0), stop=(kc == 15)),
                          waits=waits, inc=inc)
                    inc = None
                    if kc == 15:
                        inc, s3 = tick(ST)
                    em.op("tensor",
                          lambda e, kc=kc, slot=slot, b3=b3, f=f: e.matmul(
                              ps[:, 512 * b3:512 * b3 + 512],
                              w13r[:, slot, kc, 128:256], x1h[:, f, kc, :],
                              start=(kc == 0), stop=(kc == 15)),
                          inc=inc)
                w1gate[slot] = [(ST, s3)]
                inc, vsil = tick(SS)
                em.op("scalar",
                      lambda e, m=m, b1=b1: e.activation(silt[:, m % 2, :],
                                                         ps[:, 512 * b1:512 * b1 + 512], AF.Silu),
                      waits=[(ST, s1)] + ([(SV, vhm_hist[m - 2])] if m >= 2 else []), inc=inc)
                inc, vhm = tick(SV)
                em.op("vector",
                      lambda e, m=m, b3=b3: e.tensor_mul(hT[:, m, :], silt[:, m % 2, :],
                                                         ps[:, 512 * b3:512 * b3 + 512]),
                      waits=[(ST, s3), (SS, vsil)] +
                            ([(ST, f_w2_fin)] if (f == 1 and m == 0) else []),
                      inc=inc)
                vhm_hist[m] = vhm
                bank_gate[b3] = [(SV, vhm)]
                vsil_hist[m] = vsil
                if KDBG == "hpre":
                    hp_vals[m] = 0
                    bank_gate[b1] = [(SV, vhm)]
                    continue
                def emit_tr(j):
                    bj = j % 4
                    inc, vtr = tick(ST)
                    em.op("tensor",
                          lambda e, j=j, bj=bj: e.matmul(ps[:, 512 * bj:512 * bj + 512],
                                                         h128b[:, :], hT[:, j, :],
                                                         start=True, stop=True),
                          waits=[(SV, vhm_hist[j]), (SS, vsil_hist[j]), (hadS, vHAD)], inc=inc)
                    inc, hp_vals[j] = tick(SS)
                    em.op("scalar",
                          lambda e, j=j, bj=bj: e.activation(hT[:, j, :],
                                                             ps[:, 512 * bj:512 * bj + 512],
                                                             AF.Copy, scale=ISQ),
                          waits=[(ST, vtr)], inc=inc)
                    bank_gate[bj] = [(SS, hp_vals[j])]
                if m >= 1:
                    emit_tr(m - 1)
                    if not KBFBATCH and m >= 2:
                        bf_burst(m - 2)
                if m == NH - 1:
                    for _ in range(3):
                        em.op("vector", lambda e: e.tensor_add(w_rep[:, :], zeros[:, :],
                                                               zeros[:, :]))
                    inc, vsp = tick(SV)
                    em.op("vector", lambda e: e.tensor_add(w_rep[:, :], zeros[:, :],
                                                           zeros[:, :]), inc=inc)
                    vhm_hist[m] = max(vhm_hist[m], vsp)
                    emit_tr(m)
                    if KBFBATCH:
                        for _md in range(NH):
                            bf_burst(_md)
                    else:
                        bf_burst(m - 1)
                        bf_burst(m)
            dbg_dump("h", lambda: hT[:, 0:16, :], [(SS, max(hp_vals))], cast=True, n=512)
            dbg_dump("hpre", lambda: hT[:, 0:16, :], [(SV, vhm_hist[15])], cast=True, n=512)
            phys = bf_state["phys"]
            bf_fin = bf_state["fin"]
            hp_fin_prev = bf_fin
            dbg_dump("hb2", lambda: hT[:, 0:16, :], [(SV, bf_fin)], cast=True, n=512)
            chunk_retire = [None] * NH
            prev_evac = [0, 0]
            last_acc = [0] * 16
            gend = 0
            for g in range(16):
                for j in range(4 * g, 4 * g + 4):
                    slot = j % 8
                    _, pr = tick(w2S[slot])
                    em.op("gpsimd",
                          lambda e, j=j, slot=slot: e.dma_start(
                              w2r[:, slot, :], W2R[:, j, :]),
                          waits=list(w2gate[slot]), inc=(w2S[slot], 16))
                    w2gate[slot] = []
                    chunk_retire[j] = (w2S[slot], pr)
                g_stops = [0] * 16
                for mo in range(16):
                    bk = mo % 2
                    for jj in range(4):
                        j = 4 * g + jj
                        waits = []
                        if jj == 0:
                            waits = [chunk_retire[4 * g], chunk_retire[4 * g + 1],
                                     chunk_retire[4 * g + 2], chunk_retire[4 * g + 3],
                                     (SV, bf_fin), (SV, prev_evac[bk])] + dbg_gate()
                            if g == 0 and mo < 2:
                                waits += bank_gate[bk]
                        inc = None
                        if jj == 3:
                            inc, g_stops[mo] = tick(ST)
                        em.op("tensor",
                              lambda e, jj=jj, mo=mo, j=j, bk=bk, ph=phys[j]: e.matmul(
                                  ps[:, 512 * bk:512 * bk + 512],
                                  w2r[:, j % 8, 128 * mo:128 * mo + 128],
                                  hslots[ph],
                                  start=(jj == 0), stop=(jj == 3)),
                              waits=waits, inc=inc)
                    inc, v = tick(SV)
                    if g == 0:
                        em.op("vector",
                              lambda e, mo=mo, bk=bk: e.tensor_copy(
                                  accT[:, mo, :], ps[:, 512 * bk:512 * bk + 512]),
                              waits=[(ST, g_stops[mo])] +
                                    ([(outS, f_out_val)] if f == 1 and mo < 2 else []),
                              inc=inc)
                    else:
                        em.op("vector",
                              lambda e, mo=mo, bk=bk: e.tensor_add(
                                  accT[:, mo, :], accT[:, mo, :],
                                  ps[:, 512 * bk:512 * bk + 512]),
                              waits=[(ST, g_stops[mo])], inc=inc)
                    prev_evac[bk] = v
                    last_acc[mo] = v
                gend = g_stops[15]
                for j in range(4 * g, 4 * g + 4):
                    w2gate[j % 8] = [(ST, gend)]
            f_w2_fin = gend
            bank_gate[0] = [(SV, last_acc[14])]
            bank_gate[1] = [(SV, last_acc[15])]
            dbg_dump("acc", lambda: accT[:, :, :], [(SV, last_acc[15])], n=512)
            z2_vals = [0] * 16
            for mo in range(16):
                inc, z2_vals[mo] = tick(SV)
                em.op("vector",
                      lambda e, mo=mo, f=f: e.tensor_add(accT[:, mo, :], accT[:, mo, :],
                                                         x1h[:, f, mo, :]),
                      waits=[(SV, last_acc[mo])] + (dbg_gate() if mo == 0 else []), inc=inc)
            o2_vals = [0] * 16

            def ln2_writer(mu_b, rs_b, bc_ready, o2_vals=o2_vals):
                last = 0
                for mo in range(16):
                    em.op("vector",
                          lambda e, mo=mo, mu_b=mu_b: e.tensor_sub(
                              accT[:, mo, :], accT[:, mo, :], mu_b),
                          waits=[bc_ready] if mo == 0 else [])
                    inc, o2_vals[mo] = tick(SV)
                    em.op("vector",
                          lambda e, mo=mo, rs_b=rs_b: e.tensor_mul(
                              accT[:, mo, :], accT[:, mo, :], rs_b),
                          inc=inc)
                    last = o2_vals[mo]
                return last

            ln_phase(lambda mo: accT[:, mo, :], 512, (1024, 1536), (2048, 2560),
                     ln2_writer, [(SV, z2_vals[mo]) for mo in range(16)])
            _, f_out_val = tick(outS)
            em.op("sync",
                  lambda e, f=f: e.dma_start(outR[:, :, 512 * f:512 * f + 512], accT[:, :, :]),
                  waits=[(SV, o2_vals[15])], inc=(outS, 16))
        fstack.close()

        em.wait("sync", [(outS, outS.n), (spS, spS.n)])

        with nc.Block() as block:
            @block.sync
            def _(e):
                for f in em.streams["sync"]:
                    f(e)

            @block.gpsimd
            def _(e):
                for f in em.streams["gpsimd"]:
                    f(e)

            @block.tensor
            def _(e):
                for f in em.streams["tensor"]:
                    f(e)

            @block.vector
            def _(e):
                for f in em.streams["vector"]:
                    f(e)

            @block.scalar
            def _(e):
                for f in em.streams["scalar"]:
                    f(e)
    return nc


_NC_CACHE = None
LAST_RESULTS = None


def _get_nc():
    global _NC_CACHE
    if _NC_CACHE is None:
        _NC_CACHE = build_nc()
    return _NC_CACHE


def kernel(L0_attn_x, L0_attn_time_decay, L0_attn_time_first, L0_attn_tm_k, L0_attn_tm_v,
           L0_attn_tm_r, L0_attn_tm_g, L0_attn_Wr, L0_attn_Wk, L0_attn_Wv, L0_attn_Wg,
           L0_attn_Wo, L0_attn_ln_g, L0_attn_ln_b, L0_ffn_W1, L0_ffn_W2, L0_ffn_W3):
    f32 = np.float32
    x = np.asarray(L0_attn_x, f32)[0]
    xTt = np.ascontiguousarray(x.T)

    def vt(v):
        return np.ascontiguousarray(np.asarray(v, f32).reshape(16, 128).T)

    PARh = np.concatenate([vt(L0_attn_tm_k), vt(L0_attn_tm_v), vt(L0_attn_tm_r),
                           vt(L0_attn_tm_g), vt(L0_attn_time_decay), vt(L0_attn_time_first),
                           vt(L0_attn_ln_g), vt(L0_attn_ln_b)], axis=1)
    Hm = np.array([[1.0]], f32)
    while Hm.shape[0] < 128:
        Hm = np.block([[Hm, Hm], [Hm, -Hm]])
    Hm = np.ascontiguousarray(Hm.astype(f32))

    WrT = np.ascontiguousarray(np.asarray(L0_attn_Wr, f32).T)
    WkT = np.ascontiguousarray(np.asarray(L0_attn_Wk, f32).T)
    WvT = np.ascontiguousarray(np.asarray(L0_attn_Wv, f32).T)
    WgT = np.ascontiguousarray(np.asarray(L0_attn_Wg, f32).T)
    WoT = np.ascontiguousarray(np.asarray(L0_attn_Wo, f32).T)
    W1Th = np.asarray(L0_ffn_W1, f32).T.reshape(C, 64, 128)
    W3Th = np.asarray(L0_ffn_W3, f32).T.reshape(C, 64, 128)
    W13 = np.ascontiguousarray(np.concatenate([W1Th, W3Th], axis=2)).reshape(C, 64, 256)
    W2T = np.ascontiguousarray(np.asarray(L0_ffn_W2, f32).T)

    in_maps = []
    for c in range(NCORES):
        s0 = TPC * c
        if c == 0:
            sl = np.concatenate([np.zeros((C, WARM), f32), xTt[:, 1:2], xTt[:, 0:TPC]], axis=1)
            wm = np.zeros((128, 1), f32)
        else:
            sl = np.ascontiguousarray(xTt[:, s0 - WARM - 1: s0 + TPC])
            wm = np.ones((128, 1), f32)
        in_maps.append({
            "xT": sl, "WrT": WrT, "WkT": WkT, "WvT": WvT, "WgT": WgT, "WoT": WoT,
            "W13": W13, "W2T": W2T,
            "PAR": np.ascontiguousarray(np.concatenate([PARh, wm], axis=1)),
            "HAD": Hm,
        })

    nc = _get_nc()
    res = run_bass_kernel_spmd(nc, in_maps, core_ids=list(range(NCORES)))
    global LAST_RESULTS
    LAST_RESULTS = res.results
    outs = [np.asarray(r["OUT"]) for r in res.results]
    full = np.concatenate(outs, axis=1)
    return np.ascontiguousarray(full.T)[None].astype(np.float32)


# version 06
# speedup vs baseline: 1.3197x; 1.0269x over previous
import numpy as np
import concourse.bass as bass
import concourse.mybir as mybir
from concourse.bass_utils import run_bass_kernel_spmd
from contextlib import ExitStack

F32 = mybir.dt.float32
BF16 = mybir.dt.bfloat16
AF = mybir.ActivationFunctionType
OP = mybir.AluOpType

C = 2048
T = 8192
NCORES = 8
TPC = T // NCORES
KC = 16
TA = 256
NSA = TPC // TA
WARM = 128
TF = 512
NSF = TPC // TF
H = 8192
NH = H // 128
EPS = 1e-5
ISQ = float(1.0 / np.sqrt(8192.0))
XCOLS = 1 + WARM + TPC


class Sem:
    def __init__(self, h, name, step=1):
        self.h, self.name, self.n, self.step = h, name, 0, step

    def tick(self):
        self.n += self.step
        return self.n


class Em:
    ENGS = ("sync", "gpsimd", "tensor", "vector", "scalar")

    def __init__(self):
        self.streams = {k: [] for k in self.ENGS}
        self.wm = {k: {} for k in self.ENGS}

    def op(self, eng, fn, waits=(), inc=None):
        wl = []
        wm = self.wm[eng]
        for sw in waits:
            if sw is None:
                continue
            sem, val = sw
            if val is None or val <= 0:
                continue
            if val > wm.get(sem.name, 0):
                wm[sem.name] = val
                wl.append((sem, val))

        def run(e, fn=fn, wl=wl, inc=inc):
            for sem, val in wl:
                e.wait_ge(sem.h, val)
            ins = fn(e)
            if inc is not None:
                sem, k = inc
                ins.then_inc(sem.h, k)
        self.streams[eng].append(run)

    def wait(self, eng, waits):
        wl = []
        wm = self.wm[eng]
        for sem, val in waits:
            if val and val > wm.get(sem.name, 0):
                wm[sem.name] = val
                wl.append((sem, val))
        if wl:
            def run(e, wl=wl):
                for sem, val in wl:
                    e.wait_ge(sem.h, val)
            self.streams[eng].append(run)


import os
KDBG = os.environ.get("KDBG", "")
KBFBATCH = bool(os.environ.get("KBFBATCH", ""))


def build_nc():
    nc = bass.Bass("TRN2", target_bir_lowering=False)
    xT = nc.dram_tensor("xT", [C, XCOLS], F32, kind="ExternalInput").ap()
    WrT = nc.dram_tensor("WrT", [C, C], F32, kind="ExternalInput").ap()
    WkT = nc.dram_tensor("WkT", [C, C], F32, kind="ExternalInput").ap()
    WvT = nc.dram_tensor("WvT", [C, C], F32, kind="ExternalInput").ap()
    WgT = nc.dram_tensor("WgT", [C, C], F32, kind="ExternalInput").ap()
    WoT = nc.dram_tensor("WoT", [C, C], F32, kind="ExternalInput").ap()
    W13 = nc.dram_tensor("W13", [C, 64, 256], F32, kind="ExternalInput").ap()
    W2T = nc.dram_tensor("W2T", [H, C], F32, kind="ExternalInput").ap()
    PAR = nc.dram_tensor("PAR", [128, 129], F32, kind="ExternalInput").ap()
    HAD = nc.dram_tensor("HAD", [128, 128], F32, kind="ExternalInput").ap()
    OUT = nc.dram_tensor("OUT", [C, TPC], F32, kind="ExternalOutput").ap()
    X1SP = nc.dram_tensor("X1SP", [C, TPC], BF16, kind="Internal").ap()
    WBK = {n: nc.dram_tensor("WB_" + n, [C, C], BF16, kind="Internal").ap()
           for n in ["r", "k", "v", "g", "o"]}

    xTr = xT.rearrange("(c p) t -> p c t", p=128)
    outR = OUT.rearrange("(c p) t -> p c t", p=128)
    x1R = X1SP.rearrange("(c p) t -> p c t", p=128)
    W13R = W13.rearrange("(c p) m n -> p c m n", p=128)
    W2R = W2T.rearrange("(c p) n -> p c n", p=128)

    em = Em()
    dbg_state = {"ap": None, "gate": None, "used": False}
    if KDBG:
        dbg_state["ap"] = nc.dram_tensor("DBG", [128, 16, 512], F32, kind="ExternalOutput").ap()
    with ExitStack() as stack:
        ent = stack.enter_context
        def dsem(name):
            return Sem(ent(nc.semaphore(name)), name, 16)
        wringS = [dsem("A%d" % i) for i in range(4)]
        w1S = [dsem("B%d" % i) for i in range(3)]
        w2S = [dsem("C%d" % i) for i in range(8)]
        xtS = dsem("XT")
        xfS = dsem("XF")
        x1S = dsem("X1")
        spS = dsem("SP")
        outS = dsem("OU")
        parS = dsem("PA")
        hadS = dsem("HA")
        dbgS = dsem("DB")
        wbS = dsem("WB")
        ST = Sem(ent(nc.semaphore("ST")), "ST")
        SV = Sem(ent(nc.semaphore("SV")), "SV")
        SS = Sem(ent(nc.semaphore("SS")), "SS")

        def tick(sem):
            v = sem.tick()
            return (sem, sem.step), v

        def dbg_dump(tag, src_fn, waits, cast=False, n=256):
            if KDBG != tag or dbg_state["used"]:
                return
            dbg_state["used"] = True
            _, v = tick(dbgS)
            em.op("gpsimd" if cast else "sync",
                  lambda e: e.dma_start(dbg_state["ap"][:, :, 0:n], src_fn()),
                  waits=waits, inc=(dbgS, 16))
            dbg_state["gate"] = (dbgS, v)

        def dbg_gate():
            g = dbg_state["gate"]
            return [g] if g is not None else []

        ps = ent(nc.psum_tensor("ps", [128, 4096], F32))
        h128b = ent(nc.sbuf_tensor("h128b", [128, 128], BF16))
        par = ent(nc.sbuf_tensor("par", [128, 129], F32))
        cns = ent(nc.sbuf_tensor("cns", [128, 48], F32))
        zeros = ent(nc.sbuf_tensor("zeros", [128, 256], F32))
        w_rep = ent(nc.sbuf_tensor("w_rep", [128, 256], F32))
        carA = ent(nc.sbuf_tensor("carA", [128, 16], F32))
        carB = ent(nc.sbuf_tensor("carB", [128, 16], F32))
        statf = ent(nc.sbuf_tensor("statf", [1, 4, 512], F32))
        statb = ent(nc.sbuf_tensor("statb", [1, 2, 512], BF16))
        lnrA = ent(nc.sbuf_tensor("lnrA", [128, 2, 512], BF16))
        lnrB = ent(nc.sbuf_tensor("lnrB", [128, 2, 512], BF16))

        tm = {k: par[:, 16 * i:16 * (i + 1)] for i, k in
              enumerate(["k", "v", "r", "g", "td", "tf", "lng", "lnb"])}
        w_c = cns[:, 0:16]
        eu_c = cns[:, 16:32]
        c2_c = cns[:, 32:48]

        wmsk = par[:, 128:129]
        _, vPAR = tick(parS)
        em.op("sync", lambda e: e.dma_start(par[:, :], PAR), inc=(parS, 16))
        vWM = vPAR
        _, vHAD = tick(hadS)
        em.op("gpsimd", lambda e: e.dma_start(h128b[:, :], HAD), inc=(hadS, 16))
        ones_col = h128b[:, 0:1]
        ones_row = h128b[0:1, :]

        em.op("scalar", lambda e: e.activation(cns[:, 0:16], tm["td"], AF.Exp),
              waits=[(parS, vPAR)])
        em.op("scalar", lambda e: e.activation(cns[:, 0:16], cns[:, 0:16], AF.Exp, scale=-1.0))
        inc, vEU = tick(SS)
        em.op("scalar", lambda e: e.activation(cns[:, 16:32], tm["tf"], AF.Exp), inc=inc)
        em.op("vector", lambda e: e.tensor_mul(cns[:, 32:48], cns[:, 0:16], cns[:, 16:32]),
              waits=[(SS, vEU)])
        inc, vZ = tick(SV)
        em.op("vector", lambda e: e.memset(zeros[:, :], 0.0), inc=inc)
        em.op("vector", lambda e: e.memset(w_rep[:, :], 0.0))
        inc, vC2 = tick(SV)
        em.op("vector", lambda e: e.tensor_scalar(cns[:, 32:48], cns[:, 32:48], -1.0, 1.0,
                                                  OP.mult, OP.add), inc=inc)

        bank_gate = [[] for _ in range(8)]
        last_ln_roll = [0]

        def ln_phase(src_fn, N, st_cols, bc_cols, out_writer, src_ready):
            cp_vals = [0] * 16
            rfv = [0] * 16
            for mo in range(16):
                w = [src_ready[mo]] if src_ready else []
                if mo >= 2:
                    w.append((ST, rfv[mo - 2]))
                else:
                    w.append((ST, last_ln_roll[0]))
                em.op("scalar",
                      lambda e, mo=mo: e.activation(lnrA[:, mo % 2, 0:N], src_fn(mo), AF.Copy),
                      waits=w)
                inc, cp_vals[mo] = tick(SS)
                em.op("scalar",
                      lambda e, mo=mo: e.activation(lnrB[:, mo % 2, 0:N], src_fn(mo), AF.Square),
                      inc=inc)
            for mo in range(16):
                waits = [(SS, cp_vals[mo])]
                if mo == 0:
                    waits += bank_gate[st_cols[0] // 512] + bank_gate[st_cols[1] // 512]
                sb_same = (st_cols[0] // 512) == (st_cols[1] // 512)
                em.op("tensor",
                      lambda e, mo=mo: e.matmul(ps[0:1, st_cols[0]:st_cols[0] + N], ones_col,
                                                lnrA[:, mo % 2, 0:N],
                                                start=(mo == 0), stop=(mo == 15)),
                      waits=waits)
                inc, rfv[mo] = tick(ST)
                em.op("tensor",
                      lambda e, mo=mo, sb_same=sb_same: e.matmul(
                          ps[0:1, st_cols[1]:st_cols[1] + N], ones_col,
                          lnrB[:, mo % 2, 0:N],
                          start=(mo == 0 and not sb_same), stop=(mo == 15)),
                      inc=inc)
            last_ln_roll[0] = rfv[15]
            st_stop = rfv[15]
            em.op("vector",
                  lambda e: e.tensor_scalar(statf[0:1, 0, 0:N], ps[0:1, st_cols[0]:st_cols[0] + N],
                                            1.0 / 2048.0, None, OP.mult),
                  waits=[(ST, st_stop)])
            em.op("vector",
                  lambda e: e.tensor_scalar(statf[0:1, 1, 0:N], ps[0:1, st_cols[1]:st_cols[1] + N],
                                            1.0 / 2048.0, EPS, OP.mult, OP.add))
            em.op("vector", lambda e: e.tensor_mul(statf[0:1, 2, 0:N], statf[0:1, 0, 0:N],
                                                   statf[0:1, 0, 0:N]))
            inc, vmunge = tick(SV)
            em.op("vector", lambda e: e.tensor_sub(statf[0:1, 1, 0:N], statf[0:1, 1, 0:N],
                                                   statf[0:1, 2, 0:N]), inc=inc)
            bank_gate[st_cols[0] // 512] = [(SV, vmunge)]
            bank_gate[st_cols[1] // 512] = [(SV, vmunge)]
            em.op("scalar",
                  lambda e: e.activation(statf[0:1, 3, 0:N], statf[0:1, 1, 0:N], AF.Ln),
                  waits=[(SV, vmunge)])
            em.op("scalar", lambda e: e.activation(statf[0:1, 3, 0:N], statf[0:1, 3, 0:N],
                                                   AF.Exp, scale=-0.5))
            em.op("scalar", lambda e: e.activation(statb[0:1, 0, 0:N], statf[0:1, 0, 0:N], AF.Copy))
            inc, vcast = tick(SS)
            em.op("scalar", lambda e: e.activation(statb[0:1, 1, 0:N], statf[0:1, 3, 0:N], AF.Copy),
                  inc=inc)
            em.op("tensor",
                  lambda e: e.matmul(ps[:, bc_cols[0]:bc_cols[0] + N], ones_row, statb[0:1, 0, 0:N],
                                     start=True, stop=True),
                  waits=[(SS, vcast), (hadS, vHAD)] + bank_gate[bc_cols[0] // 512])
            inc, vbc = tick(ST)
            bc_same = (bc_cols[0] // 512) == (bc_cols[1] // 512)
            em.op("tensor",
                  lambda e, bc_same=bc_same: e.matmul(
                      ps[:, bc_cols[1]:bc_cols[1] + N], ones_row, statb[0:1, 1, 0:N],
                      start=(not bc_same), stop=True),
                  waits=bank_gate[bc_cols[1] // 512], inc=inc)
            mu_b = ps[:, bc_cols[0]:bc_cols[0] + N]
            rs_b = ps[:, bc_cols[1]:bc_cols[1] + N]
            last = out_writer(mu_b, rs_b, (ST, vbc))
            bank_gate[bc_cols[0] // 512] = [(SV, last)]
            bank_gate[bc_cols[1] // 512] = [(SV, last)]

        astack = ExitStack()
        aent = astack.enter_context
        xtile = aent(nc.sbuf_tensor("xtile", [128, 16, 385], BF16))
        xf32 = aent(nc.sbuf_tensor("xf32", [128, 16, 256], F32))
        xx_r = aent(nc.sbuf_tensor("xx_r", [128, 256], F32))
        stg_k = aent(nc.sbuf_tensor("stg_k", [128, 16, 256], BF16))
        stg_v = aent(nc.sbuf_tensor("stg_v", [128, 16, 256], BF16))
        stg_r = aent(nc.sbuf_tensor("stg_r", [128, 16, 256], BF16))
        stg_g = aent(nc.sbuf_tensor("stg_g", [128, 16, 256], BF16))
        ekT = aent(nc.sbuf_tensor("ekT", [128, 16, 256], F32))
        uT = aent(nc.sbuf_tensor("uT", [128, 16, 256], F32))
        apg = aent(nc.sbuf_tensor("apg", [128, 2, 257], F32))
        bpg = aent(nc.sbuf_tensor("bpg", [128, 2, 257], F32))
        numb = aent(nc.sbuf_tensor("numb", [128, 2, 256], F32))
        denb = aent(nc.sbuf_tensor("denb", [128, 2, 256], F32))
        invb = aent(nc.sbuf_tensor("invb", [128, 2, 256], F32))
        rsig = aent(nc.sbuf_tensor("rsig", [128, 16, 256], BF16))
        gsr = aent(nc.sbuf_tensor("gsr", [128, 2, 256], BF16))
        wring = aent(nc.sbuf_tensor("wring", [128, 4, 2, 2048], BF16))
        attn_b16 = stg_r
        x1sub = stg_g

        wring_gate = [[] for _ in range(4)]
        spill_vals = []

        wb_pass = {n: 0 for n in WBK}

        def proj(wname, wdram, rhs_fn, N, rhs_ready):
            first_pass = (wb_pass[wname] == 0)
            wb_pass[wname] += 1
            wR = wdram.rearrange("(c p) n -> p c n", p=128)
            wbR = WBK[wname].rearrange("(c p) n -> p c n", p=128)
            dv = [0] * 8

            def emit_dma(p, waits):
                slot = p % 4
                _, dv[p] = tick(wringS[slot])
                if first_pass:
                    em.op("gpsimd",
                          lambda e, p=p, slot=slot, wR=wR: e.dma_start(
                              wring[:, slot, :, :], wR[:, 2 * p:2 * p + 2, :]),
                          waits=waits, inc=(wringS[slot], 16))
                    _, vst = tick(wbS)
                    em.op("sync",
                          lambda e, p=p, slot=slot, wbR=wbR: e.dma_start(
                              wbR[:, 2 * p:2 * p + 2, :], wring[:, slot, :, :]),
                          waits=[(wringS[slot], dv[p])], inc=(wbS, 16))
                    wring_gate[slot] = wring_gate[slot] + [(wbS, vst)]
                else:
                    em.op("sync",
                          lambda e, p=p, slot=slot, wbR=wbR: e.dma_start(
                              wring[:, slot, :, :], wbR[:, 2 * p:2 * p + 2, :]),
                          waits=waits + [(wbS, wbS.n)], inc=(wringS[slot], 16))

            for p in range(4):
                emit_dma(p, list(wring_gate[p]))
                wring_gate[p] = []
            stop_vals = [0] * 16
            kc_tick = [0] * 16
            for kc in range(16):
                p, i = kc // 2, kc % 2
                slot = p % 4
                for mo in range(16):
                    waits = []
                    if mo == 0:
                        waits.append((wringS[slot], dv[p]))
                        if kc == 0:
                            waits += list(rhs_ready)
                    if kc == 0:
                        waits += bank_gate[mo // 2]
                    inc = None
                    if kc == 15:
                        inc, stop_vals[mo] = tick(ST)
                    elif kc % 2 == 1 and mo == 15:
                        inc, kc_tick[kc] = tick(ST)
                    em.op("tensor",
                          lambda e, kc=kc, i=i, slot=slot, mo=mo, N=N, rhs_fn=rhs_fn: e.matmul(
                              ps[:, 256 * mo:256 * mo + N],
                              wring[:, slot, i, 128 * mo:128 * mo + 128],
                              rhs_fn(kc),
                              start=(kc == 0 and mo % 2 == 0), stop=(kc == 15)),
                          waits=waits, inc=inc)
                if kc in (1, 3, 5, 7):
                    emit_dma(4 + (kc - 1) // 2, [(ST, kc_tick[kc])])
            for b in range(8):
                bank_gate[b] = []
            wring_gate[0] = wring_gate[0] + [(ST, kc_tick[9])]
            wring_gate[1] = wring_gate[1] + [(ST, kc_tick[11])]
            wring_gate[2] = wring_gate[2] + [(ST, kc_tick[13])]
            wring_gate[3] = wring_gate[3] + [(ST, stop_vals[15])]
            return stop_vals

        def lerp_phase(base, n, sigs, xt_ready, extra=()):
            fin = 0
            for kc in range(16):
                w = list(extra) + [xt_ready] if kc == 0 else []
                em.op("vector",
                      lambda e, kc=kc, base=base, n=n: e.tensor_sub(
                          xx_r[:, 0:n], xtile[:, kc, base:base + n],
                          xtile[:, kc, base + 1:base + 1 + n]),
                      waits=w)
                for name, dst in sigs:
                    inc, fin = tick(SV)
                    em.op("vector",
                          lambda e, kc=kc, base=base, n=n, name=name, dst=dst:
                          e.scalar_tensor_tensor(
                              dst[:, kc, 0:n], xx_r[:, 0:n], tm[name][:, kc:kc + 1],
                              xtile[:, kc, base + 1:base + 1 + n], OP.mult, OP.add),
                          inc=inc)
            return fin

        _, vXT = tick(xtS)
        em.op("gpsimd", lambda e: e.dma_start(xtile[:, :, 0:385], xTr[:, :, 0:385]),
              inc=(xtS, 16))
        xt_ready = (xtS, vXT)

        lerp_fin = lerp_phase(0, 128, [("k", stg_k), ("v", stg_v)], xt_ready)
        kstops = proj("k", WkT, lambda kc: stg_k[:, kc, 0:128], 128, [(SV, lerp_fin)])
        ek_vals = [0] * 16
        for mo in range(16):
            inc, ek_vals[mo] = tick(SS)
            em.op("scalar",
                  lambda e, mo=mo: e.activation(ekT[:, mo, 0:128],
                                                ps[:, 256 * mo:256 * mo + 128], AF.Exp),
                  waits=[(ST, kstops[mo | 1])], inc=inc)
            bank_gate[mo // 2] = [(SS, ek_vals[mo])]
        vstops = proj("v", WvT, lambda kc: stg_v[:, kc, 0:128], 128, [(SV, lerp_fin)])
        u_vals = [0] * 16
        for mo in range(16):
            inc, u_vals[mo] = tick(SV)
            em.op("vector",
                  lambda e, mo=mo: e.tensor_mul(uT[:, mo, 0:128],
                                                ps[:, 256 * mo:256 * mo + 128],
                                                ekT[:, mo, 0:128]),
                  waits=[(ST, vstops[mo | 1]), (SS, ek_vals[mo])], inc=inc)
            bank_gate[mo // 2] = [(SV, u_vals[mo])]
        def pro_stage(mo):
            g = mo % 2
            em.op("vector",
                  lambda e, mo=mo: e.scalar_tensor_tensor(
                      w_rep[:, 0:128], zeros[:, 0:128], w_c[:, mo:mo + 1],
                      zeros[:, 0:128], OP.add, OP.add),
                  waits=[(SV, u_vals[mo]), (SV, vZ), (SV, vC2)])
            em.op("vector",
                  lambda e, mo=mo, g=g: e.tensor_tensor_scan(
                      apg[:, g, 0:128], w_rep[:, 0:128], uT[:, mo, 0:128], 0.0,
                      OP.mult, OP.add))
            em.op("vector",
                  lambda e, mo=mo, g=g: e.tensor_tensor_scan(
                      bpg[:, g, 0:128], w_rep[:, 0:128], ekT[:, mo, 0:128], 0.0,
                      OP.mult, OP.add))

        def pro_consume(mo):
            g = mo % 2
            em.op("vector", lambda e, mo=mo, g=g: e.tensor_copy(carA[:, mo:mo + 1],
                                                                apg[:, g, 127:128]))
            em.op("vector", lambda e, mo=mo, g=g: e.tensor_copy(carB[:, mo:mo + 1],
                                                                bpg[:, g, 127:128]))

        pro_stage(0)
        for mo in range(1, 16):
            pro_stage(mo)
            pro_consume(mo - 1)
        pro_consume(15)
        em.op("vector", lambda e: e.tensor_scalar(carA[:, :], carA[:, :], wmsk[:, 0:1],
                                                  None, OP.mult),
              waits=[(parS, vWM)])
        inc, vCAR = tick(SV)
        em.op("vector", lambda e: e.tensor_scalar(carB[:, :], carB[:, :], wmsk[:, 0:1],
                                                  None, OP.mult), inc=inc)

        prev_z_fin = 0
        prev_attnmul_fin = vCAR
        prev_uT_guard = vCAR
        prev_lerp_fin = lerp_fin
        pro_stop = vstops[15]
        for s in range(NSA):
            base = 128 if s == 0 else 0
            if s > 0:
                _, vXT = tick(xtS)
                em.op("gpsimd",
                      lambda e, s=s: e.dma_start(xtile[:, :, 0:257],
                                                 xTr[:, :, 128 + 256 * s:385 + 256 * s]),
                      waits=[(SV, prev_lerp_fin)], inc=(xtS, 16))
                xt_ready = (xtS, vXT)
            _, vXF = tick(xfS)
            em.op("sync",
                  lambda e, s=s: e.dma_start(xf32[:, :, :],
                                             xTr[:, :, 129 + 256 * s:385 + 256 * s]),
                  waits=[(SV, prev_z_fin)], inc=(xfS, 16))
            extra = [(ST, pro_stop)] if s == 0 else [(spS, spill_vals[-1])]
            lerp_fin = lerp_phase(base, 256,
                                  [("k", stg_k), ("v", stg_v), ("r", stg_r), ("g", stg_g)],
                                  xt_ready, extra=extra)
            prev_lerp_fin = lerp_fin
            kstops = proj("k", WkT, lambda kc: stg_k[:, kc, 0:256], 256, [(SV, lerp_fin)])
            ek_vals = [0] * 16
            for mo in range(16):
                inc, ek_vals[mo] = tick(SS)
                em.op("scalar",
                      lambda e, mo=mo: e.activation(ekT[:, mo, 0:256],
                                                    ps[:, 256 * mo:256 * mo + 256], AF.Exp),
                      waits=[(ST, kstops[mo | 1]), (SV, prev_attnmul_fin)], inc=inc)
                bank_gate[mo // 2] = [(SS, ek_vals[mo])]
            vstops = proj("v", WvT, lambda kc: stg_v[:, kc, 0:256], 256, [(SV, lerp_fin)])
            u_vals = [0] * 16
            for mo in range(16):
                inc, u_vals[mo] = tick(SV)
                em.op("vector",
                      lambda e, mo=mo: e.tensor_mul(uT[:, mo, 0:256],
                                                    ps[:, 256 * mo:256 * mo + 256],
                                                    ekT[:, mo, 0:256]),
                      waits=[(ST, vstops[mo | 1]), (SS, ek_vals[mo]), (SV, prev_uT_guard)],
                      inc=inc)
                bank_gate[mo // 2] = [(SV, u_vals[mo])]
            dbg_dump("stgk", lambda: stg_k[:, :, 0:256].rearrange("p c t -> p c t"),
                     [(SV, lerp_fin)], cast=True)
            dbg_dump("ek", lambda: ekT[:, :, :], [(SS, ek_vals[15])])
            dbg_dump("u", lambda: uT[:, :, :], [(SV, u_vals[15])])
            wkv_vals = [0] * 16
            vinv_hist = [0] * 16

            def sc_stage(mo, s=s):
                g = mo % 2
                em.op("vector",
                      lambda e, mo=mo: e.scalar_tensor_tensor(
                          w_rep[:, 0:256], zeros[:, 0:256], w_c[:, mo:mo + 1],
                          zeros[:, 0:256], OP.add, OP.add),
                      waits=[(SV, u_vals[mo])] + ([(SV, vCAR)] if (s == 0 and mo == 0) else []))
                em.op("vector", lambda e, mo=mo, g=g: e.tensor_copy(apg[:, g, 0:1],
                                                                    carA[:, mo:mo + 1]))
                em.op("vector",
                      lambda e, mo=mo, g=g: e.tensor_tensor_scan(
                          apg[:, g, 1:257], w_rep[:, 0:256], uT[:, mo, 0:256],
                          carA[:, mo:mo + 1], OP.mult, OP.add))
                em.op("vector", lambda e, mo=mo, g=g: e.tensor_copy(bpg[:, g, 0:1],
                                                                    carB[:, mo:mo + 1]))
                em.op("vector",
                      lambda e, mo=mo, g=g: e.tensor_tensor_scan(
                          bpg[:, g, 1:257], w_rep[:, 0:256], ekT[:, mo, 0:256],
                          carB[:, mo:mo + 1], OP.mult, OP.add))

            def sc_consume(mo):
                g = mo % 2
                em.op("vector", lambda e, mo=mo, g=g: e.tensor_copy(carA[:, mo:mo + 1],
                                                                    apg[:, g, 256:257]))
                em.op("vector", lambda e, mo=mo, g=g: e.tensor_copy(carB[:, mo:mo + 1],
                                                                    bpg[:, g, 256:257]))
                em.op("vector",
                      lambda e, mo=mo, g=g: e.tensor_scalar(
                          numb[:, g, :], apg[:, g, 1:257], eu_c[:, mo:mo + 1], None, OP.mult))
                em.op("vector",
                      lambda e, mo=mo, g=g: e.scalar_tensor_tensor(
                          numb[:, g, :], apg[:, g, 0:256], c2_c[:, mo:mo + 1],
                          numb[:, g, :], OP.mult, OP.add))
                em.op("vector",
                      lambda e, mo=mo, g=g: e.tensor_scalar(
                          denb[:, g, :], bpg[:, g, 1:257], eu_c[:, mo:mo + 1], None, OP.mult),
                      waits=([(SS, vinv_hist[mo - 2])] if mo >= 2 else []))
                inc, vden = tick(SV)
                em.op("vector",
                      lambda e, mo=mo, g=g: e.scalar_tensor_tensor(
                          denb[:, g, :], bpg[:, g, 0:256], c2_c[:, mo:mo + 1],
                          denb[:, g, :], OP.mult, OP.add), inc=inc)
                em.op("scalar",
                      lambda e, mo=mo, g=g: e.activation(invb[:, g, :], denb[:, g, :], AF.Ln),
                      waits=[(SV, vden)])
                inc, vinv = tick(SS)
                vinv_hist[mo] = vinv
                em.op("scalar",
                      lambda e, mo=mo, g=g: e.activation(invb[:, g, :], invb[:, g, :],
                                                         AF.Exp, scale=-1.0), inc=inc)
                inc, wkv_vals[mo] = tick(SV)
                em.op("vector",
                      lambda e, mo=mo, g=g: e.tensor_mul(ekT[:, mo, 0:256], numb[:, g, :],
                                                         invb[:, g, :]),
                      waits=[(SS, vinv)] + (dbg_gate() if mo == 0 else []), inc=inc)

            sc_stage(0)
            for mo in range(1, 16):
                sc_stage(mo)
                sc_consume(mo - 1)
            sc_consume(15)
            dbg_dump("wkv", lambda: ekT[:, :, :], [(SV, wkv_vals[15])])
            rstops = proj("r", WrT, lambda kc: stg_r[:, kc, 0:256], 256, [(SV, lerp_fin)])
            rs_vals = [0] * 16
            for mo in range(16):
                inc, rs_vals[mo] = tick(SS)
                em.op("scalar",
                      lambda e, mo=mo: e.activation(rsig[:, mo, 0:256],
                                                    ps[:, 256 * mo:256 * mo + 256], AF.Sigmoid),
                      waits=[(ST, rstops[mo | 1])], inc=inc)
                bank_gate[mo // 2] = [(SS, rs_vals[mo])]
            am_vals = [0] * 16
            for mo in range(16):
                inc, am_vals[mo] = tick(SV)
                em.op("vector",
                      lambda e, mo=mo: e.tensor_mul(ekT[:, mo, 0:256], ekT[:, mo, 0:256],
                                                    rsig[:, mo, 0:256]),
                      waits=[(SS, rs_vals[mo]), (SV, wkv_vals[mo])] +
                            (dbg_gate() if mo == 0 else []), inc=inc)
            gstops = proj("g", WgT, lambda kc: stg_g[:, kc, 0:256], 256, [(SV, lerp_fin)])
            ab_vals = [0] * 16
            for mo in range(16):
                inc, vgs = tick(SS)
                em.op("scalar",
                      lambda e, mo=mo: e.activation(gsr[:, mo % 2, :],
                                                    ps[:, 256 * mo:256 * mo + 256], AF.Sigmoid),
                      waits=[(ST, gstops[mo | 1])] +
                            ([(SV, ab_vals[mo - 2])] if mo >= 2 else []), inc=inc)
                bank_gate[mo // 2] = [(SS, vgs)]
                inc, ab_vals[mo] = tick(SV)
                em.op("vector",
                      lambda e, mo=mo: e.tensor_mul(attn_b16[:, mo, 0:256], ekT[:, mo, 0:256],
                                                    gsr[:, mo % 2, :]),
                      waits=[(SS, vgs), (SV, am_vals[mo])], inc=inc)
            prev_attnmul_fin = ab_vals[15]
            dbg_dump("attnin", lambda: attn_b16[:, :, 0:256], [(SV, ab_vals[15])], cast=True)
            ostops = proj("o", WoT, lambda kc: attn_b16[:, kc, 0:256], 256, [(SV, ab_vals[15])])
            o_vals = [0] * 16
            for mo in range(16):
                inc, o_vals[mo] = tick(SS)
                em.op("scalar",
                      lambda e, mo=mo: e.activation(uT[:, mo, 0:256],
                                                    ps[:, 256 * mo:256 * mo + 256], AF.Copy),
                      waits=[(ST, ostops[mo | 1])], inc=inc)
                bank_gate[mo // 2] = [(SS, o_vals[mo])]
            dbg_dump("o", lambda: uT[:, :, :], [(SS, o_vals[15])])
            z_vals = [0] * 16

            def attn_writer(mu_b, rs_b, bc_ready, vXF=vXF, z_vals=z_vals):
                last = 0
                for mo in range(16):
                    em.op("vector",
                          lambda e, mo=mo, mu_b=mu_b: e.tensor_sub(
                              numb[:, mo % 2, :], uT[:, mo, 0:256], mu_b),
                          waits=[bc_ready] if mo == 0 else [])
                    em.op("vector",
                          lambda e, mo=mo, rs_b=rs_b: e.tensor_mul(
                              numb[:, mo % 2, :], numb[:, mo % 2, :], rs_b))
                    em.op("vector",
                          lambda e, mo=mo: e.tensor_scalar(
                              numb[:, mo % 2, :], numb[:, mo % 2, :],
                              tm["lng"][:, mo:mo + 1], tm["lnb"][:, mo:mo + 1],
                              OP.mult, OP.add))
                    inc, z_vals[mo] = tick(SV)
                    em.op("vector",
                          lambda e, mo=mo: e.tensor_add(uT[:, mo, 0:256], numb[:, mo % 2, :],
                                                        xf32[:, mo, :]),
                          waits=[(xfS, vXF)] + (dbg_gate() if mo == 0 else []), inc=inc)
                    last = z_vals[mo]
                return last

            ln_phase(lambda mo: uT[:, mo, 0:256], 256, (0, 256), (512, 768),
                     attn_writer, [(SS, o_vals[mo]) for mo in range(16)])
            prev_z_fin = z_vals[15]
            x1_vals = [0] * 16

            def ln1_writer(mu_b, rs_b, bc_ready, x1_vals=x1_vals, gstop=gstops[15]):
                last = 0
                for mo in range(16):
                    em.op("vector",
                          lambda e, mo=mo, mu_b=mu_b: e.tensor_sub(
                              numb[:, mo % 2, :], uT[:, mo, 0:256], mu_b),
                          waits=[bc_ready] if mo == 0 else [])
                    inc, x1_vals[mo] = tick(SV)
                    em.op("vector",
                          lambda e, mo=mo, rs_b=rs_b: e.tensor_mul(
                              x1sub[:, mo, 0:256], numb[:, mo % 2, :], rs_b),
                          waits=[(ST, gstop)], inc=inc)
                    last = x1_vals[mo]
                return last

            dbg_dump("z", lambda: uT[:, :, :], [(SV, z_vals[15])])
            ln_phase(lambda mo: uT[:, mo, 0:256], 256, (1024, 1280), (1536, 1792),
                     ln1_writer, [(SV, z_vals[mo]) for mo in range(16)])
            prev_uT_guard = x1_vals[15]
            dbg_dump("x1", lambda: x1sub[:, :, 0:256], [(SV, x1_vals[15])], cast=True)
            prev_sp = [(spS, spill_vals[-1])] if spill_vals else []
            _, vSP = tick(spS)
            em.op("sync",
                  lambda e, s=s: e.dma_start(x1R[:, :, 256 * s:256 * s + 256],
                                             x1sub[:, :, 0:256]),
                  waits=[(SV, x1_vals[15])] + dbg_gate() + prev_sp, inc=(spS, 16))
            spill_vals.append(vSP)

        attn_fin = {"ST": ST.n, "SV": SV.n, "SS": SS.n}
        astack.close()

        fstack = ExitStack()
        fent = fstack.enter_context
        x1h = fent(nc.sbuf_tensor("x1h", [128, 2, 16, 512], BF16))
        hT = fent(nc.sbuf_tensor("hT", [128, 64, 512], BF16))
        w13r = fent(nc.sbuf_tensor("w13r", [128, 3, 16, 256], BF16))
        silt = fent(nc.sbuf_tensor("silt", [128, 2, 512], BF16))
        w2r = fent(nc.sbuf_tensor("w2r", [128, 8, 2048], BF16))
        accT = fent(nc.sbuf_tensor("accT", [128, 16, 512], F32))
        hscr = fent(nc.sbuf_tensor("hscr", [128, 512], BF16))

        barrier = [(ST, attn_fin["ST"]), (SV, attn_fin["SV"]), (SS, attn_fin["SS"])]
        hslots = [hT[:, j, :] for j in range(NH)] + [hscr[:, :]]

        x1Rf = X1SP.rearrange("(c p) (f t) -> p f c t", p=128, f=2)
        _, x1rd_v = tick(x1S)
        em.op("sync",
              lambda e: e.dma_start(x1h[:, :, :, :], x1Rf),
              waits=barrier + [(spS, spill_vals[-1])], inc=(x1S, 16))

        w1gate = [[], [], []]
        w2gate = [[] for _ in range(8)]
        f_w2_fin = 0
        f_out_val = 0
        for f in range(NSF):
            hp_vals = [0] * NH
            vhm_hist = [0] * NH
            vsil_hist = [0] * NH
            bf_state = {"phys": list(range(NH)), "scratch": NH, "fin": 0}

            def bf_burst(mdone):
                phys = bf_state["phys"]
                first = [(SS, hp_vals[mdone])] + (dbg_gate() if KDBG == "h" else [])
                for d in (1, 2, 4, 8, 16, 32):
                    if (mdone + 1) % (2 * d) != 0:
                        continue
                    base = mdone + 1 - 2 * d
                    for j in range(base, base + d):
                        a, b = phys[j], phys[j + d]
                        em.op("vector",
                              lambda e, a=a, b=b, sc=bf_state["scratch"]: e.tensor_add(
                                  hslots[sc], hslots[a], hslots[b]),
                              waits=first)
                        first = []
                        inc, v = tick(SV)
                        em.op("vector",
                              lambda e, a=a, b=b: e.tensor_sub(hslots[b], hslots[a],
                                                               hslots[b]),
                              inc=inc)
                        phys[j] = bf_state["scratch"]
                        bf_state["scratch"] = a
                        bf_state["fin"] = v
            for m in range(NH):
                slot = m % 3
                b1 = m % 4
                b3 = 4 + (m % 4)
                _, vd3 = tick(w1S[slot])
                em.op("gpsimd",
                      lambda e, m=m, slot=slot: e.dma_start(
                          w13r[:, slot, :, :], W13R[:, :, m, :]),
                      waits=(barrier if (f == 0 and m == 0) else []) + list(w1gate[slot]),
                      inc=(w1S[slot], 16))
                s1 = s3 = 0
                for kc in range(16):
                    waits = []
                    if kc == 0:
                        waits = [(w1S[slot], vd3), (x1S, x1rd_v)] + \
                            list(bank_gate[b1]) + list(bank_gate[b3])
                        if f == 1 and m == 0:
                            waits.append((ST, f_w2_fin))
                            waits.append((outS, f_out_val))
                    inc = None
                    if kc == 15:
                        inc, s1 = tick(ST)
                    em.op("tensor",
                          lambda e, kc=kc, slot=slot, b1=b1, f=f: e.matmul(
                              ps[:, 512 * b1:512 * b1 + 512],
                              w13r[:, slot, kc, 0:128], x1h[:, f, kc, :],
                              start=(kc == 0), stop=(kc == 15)),
                          waits=waits, inc=inc)
                    inc = None
                    if kc == 15:
                        inc, s3 = tick(ST)
                    em.op("tensor",
                          lambda e, kc=kc, slot=slot, b3=b3, f=f: e.matmul(
                              ps[:, 512 * b3:512 * b3 + 512],
                              w13r[:, slot, kc, 128:256], x1h[:, f, kc, :],
                              start=(kc == 0), stop=(kc == 15)),
                          inc=inc)
                w1gate[slot] = [(ST, s3)]
                inc, vsil = tick(SS)
                em.op("scalar",
                      lambda e, m=m, b1=b1: e.activation(silt[:, m % 2, :],
                                                         ps[:, 512 * b1:512 * b1 + 512], AF.Silu),
                      waits=[(ST, s1)] + ([(SV, vhm_hist[m - 2])] if m >= 2 else []), inc=inc)
                inc, vhm = tick(SV)
                em.op("vector",
                      lambda e, m=m, b3=b3: e.tensor_mul(hT[:, m, :], silt[:, m % 2, :],
                                                         ps[:, 512 * b3:512 * b3 + 512]),
                      waits=[(ST, s3), (SS, vsil)] +
                            ([(ST, f_w2_fin)] if (f == 1 and m == 0) else []),
                      inc=inc)
                vhm_hist[m] = vhm
                bank_gate[b3] = [(SV, vhm)]
                vsil_hist[m] = vsil
                if KDBG == "hpre":
                    hp_vals[m] = 0
                    bank_gate[b1] = [(SV, vhm)]
                    continue
                def emit_tr(j):
                    bj = j % 4
                    inc, vtr = tick(ST)
                    em.op("tensor",
                          lambda e, j=j, bj=bj: e.matmul(ps[:, 512 * bj:512 * bj + 512],
                                                         h128b[:, :], hT[:, j, :],
                                                         start=True, stop=True),
                          waits=[(SV, vhm_hist[j]), (SS, vsil_hist[j]), (hadS, vHAD)], inc=inc)
                    inc, hp_vals[j] = tick(SS)
                    em.op("scalar",
                          lambda e, j=j, bj=bj: e.activation(hT[:, j, :],
                                                             ps[:, 512 * bj:512 * bj + 512],
                                                             AF.Copy, scale=ISQ),
                          waits=[(ST, vtr)], inc=inc)
                    bank_gate[bj] = [(SS, hp_vals[j])]
                if m >= 1:
                    emit_tr(m - 1)
                    if not KBFBATCH and m >= 2:
                        bf_burst(m - 2)
                if m == NH - 1:
                    for _ in range(3):
                        em.op("vector", lambda e: e.tensor_add(w_rep[:, :], zeros[:, :],
                                                               zeros[:, :]))
                    inc, vsp = tick(SV)
                    em.op("vector", lambda e: e.tensor_add(w_rep[:, :], zeros[:, :],
                                                           zeros[:, :]), inc=inc)
                    vhm_hist[m] = max(vhm_hist[m], vsp)
                    emit_tr(m)
                    if KBFBATCH:
                        for _md in range(NH):
                            bf_burst(_md)
                    else:
                        bf_burst(m - 1)
                        bf_burst(m)
            dbg_dump("h", lambda: hT[:, 0:16, :], [(SS, max(hp_vals))], cast=True, n=512)
            dbg_dump("hpre", lambda: hT[:, 0:16, :], [(SV, vhm_hist[15])], cast=True, n=512)
            phys = bf_state["phys"]
            bf_fin = bf_state["fin"]
            hp_fin_prev = bf_fin
            dbg_dump("hb2", lambda: hT[:, 0:16, :], [(SV, bf_fin)], cast=True, n=512)
            chunk_retire = [None] * NH
            prev_evac = [0, 0]
            last_acc = [0] * 16
            gend = 0
            for g in range(16):
                for j in range(4 * g, 4 * g + 4):
                    slot = j % 8
                    _, pr = tick(w2S[slot])
                    em.op("gpsimd",
                          lambda e, j=j, slot=slot: e.dma_start(
                              w2r[:, slot, :], W2R[:, j, :]),
                          waits=list(w2gate[slot]), inc=(w2S[slot], 16))
                    w2gate[slot] = []
                    chunk_retire[j] = (w2S[slot], pr)
                g_stops = [0] * 16
                for mo in range(16):
                    bk = mo % 2
                    for jj in range(4):
                        j = 4 * g + jj
                        waits = []
                        if jj == 0:
                            waits = [chunk_retire[4 * g], chunk_retire[4 * g + 1],
                                     chunk_retire[4 * g + 2], chunk_retire[4 * g + 3],
                                     (SV, bf_fin), (SV, prev_evac[bk])] + dbg_gate()
                            if g == 0 and mo < 2:
                                waits += bank_gate[bk]
                        inc = None
                        if jj == 3:
                            inc, g_stops[mo] = tick(ST)
                        em.op("tensor",
                              lambda e, jj=jj, mo=mo, j=j, bk=bk, ph=phys[j]: e.matmul(
                                  ps[:, 512 * bk:512 * bk + 512],
                                  w2r[:, j % 8, 128 * mo:128 * mo + 128],
                                  hslots[ph],
                                  start=(jj == 0), stop=(jj == 3)),
                              waits=waits, inc=inc)
                    inc, v = tick(SV)
                    if g == 0:
                        em.op("vector",
                              lambda e, mo=mo, bk=bk: e.tensor_copy(
                                  accT[:, mo, :], ps[:, 512 * bk:512 * bk + 512]),
                              waits=[(ST, g_stops[mo])] +
                                    ([(outS, f_out_val)] if f == 1 and mo < 2 else []),
                              inc=inc)
                    else:
                        em.op("vector",
                              lambda e, mo=mo, bk=bk: e.tensor_add(
                                  accT[:, mo, :], accT[:, mo, :],
                                  ps[:, 512 * bk:512 * bk + 512]),
                              waits=[(ST, g_stops[mo])], inc=inc)
                    prev_evac[bk] = v
                    last_acc[mo] = v
                gend = g_stops[15]
                for j in range(4 * g, 4 * g + 4):
                    w2gate[j % 8] = [(ST, gend)]
            f_w2_fin = gend
            bank_gate[0] = [(SV, last_acc[14])]
            bank_gate[1] = [(SV, last_acc[15])]
            dbg_dump("acc", lambda: accT[:, :, :], [(SV, last_acc[15])], n=512)
            z2_vals = [0] * 16
            for mo in range(16):
                inc, z2_vals[mo] = tick(SV)
                em.op("vector",
                      lambda e, mo=mo, f=f: e.tensor_add(accT[:, mo, :], accT[:, mo, :],
                                                         x1h[:, f, mo, :]),
                      waits=[(SV, last_acc[mo])] + (dbg_gate() if mo == 0 else []), inc=inc)
            o2_vals = [0] * 16

            def ln2_writer(mu_b, rs_b, bc_ready, o2_vals=o2_vals):
                last = 0
                for mo in range(16):
                    em.op("vector",
                          lambda e, mo=mo, mu_b=mu_b: e.tensor_sub(
                              accT[:, mo, :], accT[:, mo, :], mu_b),
                          waits=[bc_ready] if mo == 0 else [])
                    inc, o2_vals[mo] = tick(SV)
                    em.op("vector",
                          lambda e, mo=mo, rs_b=rs_b: e.tensor_mul(
                              accT[:, mo, :], accT[:, mo, :], rs_b),
                          inc=inc)
                    last = o2_vals[mo]
                return last

            ln_phase(lambda mo: accT[:, mo, :], 512, (1024, 1536), (2048, 2560),
                     ln2_writer, [(SV, z2_vals[mo]) for mo in range(16)])
            _, f_out_val = tick(outS)
            em.op("sync",
                  lambda e, f=f: e.dma_start(outR[:, :, 512 * f:512 * f + 512], accT[:, :, :]),
                  waits=[(SV, o2_vals[15])], inc=(outS, 16))
        fstack.close()

        em.wait("sync", [(outS, outS.n), (spS, spS.n)])

        with nc.Block() as block:
            @block.sync
            def _(e):
                for f in em.streams["sync"]:
                    f(e)

            @block.gpsimd
            def _(e):
                for f in em.streams["gpsimd"]:
                    f(e)

            @block.tensor
            def _(e):
                for f in em.streams["tensor"]:
                    f(e)

            @block.vector
            def _(e):
                for f in em.streams["vector"]:
                    f(e)

            @block.scalar
            def _(e):
                for f in em.streams["scalar"]:
                    f(e)
    return nc


_NC_CACHE = None
LAST_RESULTS = None


def _get_nc():
    global _NC_CACHE
    if _NC_CACHE is None:
        _NC_CACHE = build_nc()
    return _NC_CACHE


def kernel(L0_attn_x, L0_attn_time_decay, L0_attn_time_first, L0_attn_tm_k, L0_attn_tm_v,
           L0_attn_tm_r, L0_attn_tm_g, L0_attn_Wr, L0_attn_Wk, L0_attn_Wv, L0_attn_Wg,
           L0_attn_Wo, L0_attn_ln_g, L0_attn_ln_b, L0_ffn_W1, L0_ffn_W2, L0_ffn_W3):
    f32 = np.float32
    x = np.asarray(L0_attn_x, f32)[0]
    xTt = np.ascontiguousarray(x.T)

    def vt(v):
        return np.ascontiguousarray(np.asarray(v, f32).reshape(16, 128).T)

    PARh = np.concatenate([vt(L0_attn_tm_k), vt(L0_attn_tm_v), vt(L0_attn_tm_r),
                           vt(L0_attn_tm_g), vt(L0_attn_time_decay), vt(L0_attn_time_first),
                           vt(L0_attn_ln_g), vt(L0_attn_ln_b)], axis=1)
    Hm = np.array([[1.0]], f32)
    while Hm.shape[0] < 128:
        Hm = np.block([[Hm, Hm], [Hm, -Hm]])
    Hm = np.ascontiguousarray(Hm.astype(f32))

    WrT = np.ascontiguousarray(np.asarray(L0_attn_Wr, f32).T)
    WkT = np.ascontiguousarray(np.asarray(L0_attn_Wk, f32).T)
    WvT = np.ascontiguousarray(np.asarray(L0_attn_Wv, f32).T)
    WgT = np.ascontiguousarray(np.asarray(L0_attn_Wg, f32).T)
    WoT = np.ascontiguousarray(np.asarray(L0_attn_Wo, f32).T)
    W1Th = np.asarray(L0_ffn_W1, f32).T.reshape(C, 64, 128)
    W3Th = np.asarray(L0_ffn_W3, f32).T.reshape(C, 64, 128)
    W13 = np.ascontiguousarray(np.concatenate([W1Th, W3Th], axis=2)).reshape(C, 64, 256)
    W2T = np.ascontiguousarray(np.asarray(L0_ffn_W2, f32).T)

    in_maps = []
    for c in range(NCORES):
        s0 = TPC * c
        if c == 0:
            sl = np.concatenate([np.zeros((C, WARM), f32), xTt[:, 1:2], xTt[:, 0:TPC]], axis=1)
            wm = np.zeros((128, 1), f32)
        else:
            sl = np.ascontiguousarray(xTt[:, s0 - WARM - 1: s0 + TPC])
            wm = np.ones((128, 1), f32)
        in_maps.append({
            "xT": sl, "WrT": WrT, "WkT": WkT, "WvT": WvT, "WgT": WgT, "WoT": WoT,
            "W13": W13, "W2T": W2T,
            "PAR": np.ascontiguousarray(np.concatenate([PARh, wm], axis=1)),
            "HAD": Hm,
        })

    nc = _get_nc()
    res = run_bass_kernel_spmd(nc, in_maps, core_ids=list(range(NCORES)))
    global LAST_RESULTS
    LAST_RESULTS = res.results
    outs = [np.asarray(r["OUT"]) for r in res.results]
    full = np.concatenate(outs, axis=1)
    return np.ascontiguousarray(full.T)[None].astype(np.float32)
